# version 02
# baseline (speedup 1.0000x reference)
import numpy as np
import ml_dtypes
import concourse.bass as bass
import concourse.mybir as mybir
from concourse.bass_utils import run_bass_kernel_spmd

F32 = mybir.dt.float32
BF16 = mybir.dt.bfloat16
I32 = mybir.dt.int32
AF = mybir.ActivationFunctionType
ALU = mybir.AluOpType
AX = mybir.AxisListType

N_CORES = 8
D = 1024
T = 2048
NCH = D // 128
BF = ml_dtypes.bfloat16
NEG = -30000.0


class Buf:
    __slots__ = ("name", "writer", "readers")

    def __init__(self, name):
        self.name = name
        self.writer = None
        self.readers = []


class Prog:
    ENGS = ("pe", "act", "dve", "pool", "sp")
    SEMS = ("pe", "act", "dve", "pool", "sp", "poolc", "actd")

    def __init__(self, nc):
        self.nc = nc
        self.sems = {e: nc.alloc_semaphore(f"sem_{e}") for e in self.SEMS}
        self.cum = {e: 0 for e in self.SEMS}
        self.qs = {e: [] for e in self.ENGS}
        self.already = {e: {f: 0 for f in self.SEMS} for e in self.ENGS}
        self.bufs = {}

    def buf(self, name):
        if name not in self.bufs:
            self.bufs[name] = Buf(name)
        return self.bufs[name]

    def add(self, eng, emit_fn, reads=(), writes=(), inc=1, is_dma=False):
        if eng == "pool" and not is_dma:
            sem = "poolc"
        elif eng == "act" and is_dma:
            sem = "actd"
        else:
            sem = eng
        reads = [self.buf(b) if isinstance(b, str) else b for b in reads]
        writes = [self.buf(b) if isinstance(b, str) else b for b in writes]
        need = {}
        for b in reads:
            if b.writer is not None:
                e, v = b.writer
                need[e] = max(need.get(e, 0), v)
        for b in writes:
            if b.writer is not None:
                e, v = b.writer
                need[e] = max(need.get(e, 0), v)
            for (e, v) in b.readers:
                need[e] = max(need.get(e, 0), v)
        if is_dma:
            need[sem] = max(need.get(sem, 0), self.cum[sem])
        waits = []
        aw = self.already[eng]
        for e, v in sorted(need.items()):
            if v > aw[e]:
                waits.append((e, v))
                aw[e] = v
        myval = self.cum[sem] + inc
        self.cum[sem] = myval
        self.qs[eng].append((waits, emit_fn, inc, sem))
        for b in reads:
            b.readers.append((sem, myval))
        for b in writes:
            b.writer = (sem, myval)
            b.readers = []
        return (sem, myval)

    def dma(self, eng, out, in_, reads=(), writes=(), **kw):
        assert eng in ("sp", "pool", "act")
        return self.add(eng, lambda E: E.dma_start(out=out, in_=in_, **kw),
                        reads=reads, writes=writes, inc=16, is_dma=True)

    def dma_group(self, eng, items, reads=(), writes=()):
        assert eng in ("sp", "pool", "act")
        for o, i, kw in items:
            so = int(np.prod([d[1] for d in o.ap]))
            si = int(np.prod([d[1] for d in i.ap]))
            assert so == si, f"dma_group size mismatch {o.ap} vs {i.ap}"

        def emit(E):
            return [E.dma_start(out=o, in_=i, **kw) for (o, i, kw) in items]
        return self.add(eng, emit, reads=reads, writes=writes,
                        inc=16 * len(items), is_dma=True)

    def barrier(self):
        token = dict(self.cum)
        for eng in self.ENGS:
            aw = self.already[eng]
            waits = []
            own = ("poolc", "pool") if eng == "pool" else (
                ("act", "actd") if eng == "act" else (eng,))
            for e, v in token.items():
                if e not in own and v > aw[e] and v > 0:
                    waits.append((e, v))
                    aw[e] = v
            if waits:
                self.qs[eng].append((waits, None, 0, eng))

    def finalize(self):
        nc = self.nc
        engmap = {"pe": "tensor", "act": "scalar", "dve": "vector",
                  "pool": "gpsimd", "sp": "sync"}
        with nc.Block() as block:
            for e in self.ENGS:
                ops = self.qs[e]
                if not ops:
                    continue

                def make_body(ops=ops, e=e):
                    def body(E):
                        for waits, emit_fn, inc, sem in ops:
                            for (we, wv) in waits:
                                E.wait_ge(self.sems[we], wv)
                            if emit_fn is not None:
                                ins = emit_fn(E)
                                if inc > 0:
                                    if isinstance(ins, list):
                                        for i1 in ins:
                                            i1.then_inc(self.sems[sem], 16)
                                    else:
                                        ins.then_inc(self.sems[sem], inc)
                    return body
                getattr(block, engmap[e])(make_body())


class Ctx:
    def __init__(self, nc):
        self.nc = nc
        self.p = Prog(nc)
        self.ps = [nc.alloc_psum_tensor(f"ps{i}", [128, 512], F32) for i in range(7)]
        self.psb = nc.alloc_psum_tensor("psb", [128, 512], BF16)
        self._ps_rr = 0
        self.sb = {}
        self.host = {}
        self.hostvals = {}

    def psum(self):
        i = self._ps_rr % 7
        self._ps_rr += 1
        return self.ps[i], f"ps{i}"

    def sbuf(self, name, shape, dtype):
        if name not in self.sb:
            self.sb[name] = self.nc.alloc_sbuf_tensor(name, shape, dtype)
        return self.sb[name]

    def win(self, name, dtype=None):
        arr = self.hostvals[name]
        a0 = arr[0] if isinstance(arr, list) else arr
        dt_ = dtype or {np.dtype(np.float32): F32, np.dtype(np.int32): I32,
                        np.dtype(BF): BF16}[a0.dtype]
        h = self.nc.dram_tensor(name, list(a0.shape), dt_, kind="ExternalInput")
        self.host[name] = arr
        return h.ap()


def ap_dims(ap, dims):
    return bass.AP(tensor=ap.tensor, offset=ap.offset, ap=dims)



def setup_consts(ctx):
    p = ctx.p
    ctx.hostvals.update({
        "d_ones": np.ones((128, 1), np.float32),
        "d_onesrow": np.ones((1, 128), np.float32),
        "d_ident": np.eye(128, dtype=BF),
    })
    a_ones = ctx.win("d_ones")
    a_onesrow = ctx.win("d_onesrow")
    a_ident = ctx.win("d_ident")
    ones = ctx.sbuf("c_ones", [128, 1], F32)
    onesrow = ctx.sbuf("c_onesrow", [1, 128], F32)
    ident = ctx.sbuf("c_ident", [128, 128], BF16)
    p.dma("sp", ones[:], a_ones, writes=["c_ones"])
    p.dma("sp", onesrow[:], a_onesrow, writes=["c_onesrow"])
    p.dma("sp", ident[:], a_ident, writes=["c_ident"])
    ctx.ones = ones
    ctx.onesrow = onesrow
    ctx.ident = ident


class WS:

    def __init__(self, ctx, base, tw, need_xh=True):
        self.base = base
        self.tw = tw
        self.x = ctx.sbuf(base + "_x", [128, NCH * tw], F32)
        self.xh = ctx.sbuf(base + "_xh", [128, NCH * tw], BF16) if need_xh else None
        self.sq = ctx.sbuf("ln_sq", [128, 2 * 512], F32)
        self.stat = ctx.sbuf("ln_stat", [1, 4 * 512], F32)
        self.tmp = ctx.sbuf("ln_tmp", [128, 2 * 512], F32)

    def b(self, name, c=None):
        if name.startswith("stat"):
            return f"ln_{name}"
        if name in ("sq", "tmp"):
            return f"ln_{name}" + ("" if c is None else str(c))
        return f"{self.base}_{name}" + ("" if c is None else str(c))


def load_x_window(ctx, ws, x_dram_item, t0):
    p, tw = ctx.p, ws.tw
    items = [(ws.x[:, c * tw:(c + 1) * tw],
              x_dram_item[c * 128:(c + 1) * 128, t0:t0 + tw], {}) for c in range(NCH)]
    p.dma_group("sp", items, writes=[ws.b("x", c) for c in range(NCH)])


def store_x_window(ctx, ws, x_dram_item, t0):
    p, tw = ctx.p, ws.tw
    items = [(x_dram_item[c * 128:(c + 1) * 128, t0:t0 + tw],
              ws.x[:, c * tw:(c + 1) * tw], {}) for c in range(NCH)]
    p.dma_group("sp", items, reads=[ws.b("x", c) for c in range(NCH)])


def emit_ln(ctx, ws, gain=None, bias=None, gbuf=None, out_aps=None, out_bufs=None):
    p, tw = ctx.p, ws.tw
    ps, psn = ctx.psum()
    psq, psqn = ctx.psum()
    stat, sq = ws.stat, ws.sq
    s_mean = stat[0:1, 0:tw]
    s_var = stat[0:1, tw:2 * tw]
    s_t2 = stat[0:1, 2 * tw:3 * tw]
    s_t3 = stat[0:1, 3 * tw:4 * tw]
    for c in range(NCH):
        xa = ws.x[:, c * tw:(c + 1) * tw]
        sqa = sq[:, (c % 2) * tw:(c % 2) * tw + tw]
        p.add("act", lambda E, xa=xa, sqa=sqa: E.activation(sqa, xa, AF.Square),
              reads=[ws.b("x", c)], writes=[ws.b("sq", c % 2)])
        p.add("pe", lambda E, xa=xa, c=c: E.matmul(ps[0:1, 0:tw], ctx.ones[:, 0:1], xa,
                                                   start=(c == 0), stop=(c == NCH - 1)),
              reads=[ws.b("x", c), "c_ones"], writes=[psn], inc=0)
        p.add("pe", lambda E, sqa=sqa, c=c: E.matmul(psq[0:1, 0:tw], ctx.ones[:, 0:1], sqa,
                                                     start=(c == 0), stop=(c == NCH - 1)),
              reads=[ws.b("sq", c % 2), "c_ones"], writes=[psn, psqn])

    p.add("dve", lambda E: E.tensor_scalar_mul(s_mean, ps[0:1, 0:tw], 1.0 / D),
          reads=[psn], writes=[ws.b("statm")])
    p.add("dve", lambda E: E.tensor_scalar_mul(s_var, psq[0:1, 0:tw], 1.0 / D),
          reads=[psqn], writes=[ws.b("statv")])
    p.add("dve", lambda E: E.tensor_tensor(s_t2, s_mean, s_mean, op=ALU.mult),
          reads=[ws.b("statm")], writes=[ws.b("statt2")])
    p.add("dve", lambda E: E.tensor_tensor(s_var, s_var, s_t2, op=ALU.subtract),
          reads=[ws.b("statt2"), ws.b("statv")], writes=[ws.b("statv")])
    if gain is None:
        p.add("dve", lambda E: E.tensor_scalar_add(s_var, s_var, 1e-5),
              reads=[ws.b("statv")], writes=[ws.b("statv")])
    else:
        p.add("dve", lambda E: E.tensor_scalar_add(s_t2, s_var, 1e-5),
              reads=[ws.b("statv")], writes=[ws.b("statt2")])
        p.add("dve", lambda E: E.reciprocal(s_t3, s_t2),
              reads=[ws.b("statt2")], writes=[ws.b("statt3")])
        p.add("dve", lambda E: E.tensor_tensor(s_t3, s_var, s_t3, op=ALU.mult),
              reads=[ws.b("statv"), ws.b("statt3")], writes=[ws.b("statt3")])
        p.add("dve", lambda E: E.tensor_scalar_add(s_t3, s_t3, 1e-5),
              reads=[ws.b("statt3")], writes=[ws.b("statt3")])
        p.add("dve", lambda E: E.tensor_tensor(s_var, s_t2, s_t3, op=ALU.mult),
              reads=[ws.b("statt2"), ws.b("statt3")], writes=[ws.b("statv")])
    p.add("act", lambda E: E.activation(s_var, s_var, AF.Ln),
          reads=[ws.b("statv")], writes=[ws.b("statv")])
    p.add("act", lambda E: E.activation(s_var, s_var, AF.Exp, scale=-0.5),
          reads=[ws.b("statv")], writes=[ws.b("statv")])
    ps2, psn2 = ctx.psum()
    ps3, psn3 = ctx.psum()

    def bc(E):
        E.matmul(ps2[:, 0:tw], ctx.onesrow[0:1, :], s_mean, start=True, stop=True)
        return E.matmul(ps3[:, 0:tw], ctx.onesrow[0:1, :], s_var, start=True, stop=True)
    p.add("pe", bc, reads=[ws.b("statm"), ws.b("statv"), "c_onesrow"], writes=[psn2, psn3])
    for c in range(NCH):
        xa = ws.x[:, c * tw:(c + 1) * tw]
        if out_aps is None:
            dst, dstb = ws.xh[:, c * tw:(c + 1) * tw], ws.b("xh", c)
        else:
            dst, dstb = out_aps[c], out_bufs[c]
        tm = ws.tmp[:, (c % 2) * tw:(c % 2) * tw + tw]

        p.add("dve", lambda E, xa=xa, tm=tm: E.tensor_tensor(tm, xa, ps2[:, 0:tw], op=ALU.subtract),
              reads=[ws.b("x", c), psn2], writes=[ws.b("tmp", c % 2)])
        if gain is None:
            p.add("dve", lambda E, dst=dst, tm=tm: E.tensor_tensor(dst, tm, ps3[:, 0:tw], op=ALU.mult),
                  reads=[ws.b("tmp", c % 2), psn3], writes=[dstb])
        else:
            p.add("dve", lambda E, tm=tm: E.tensor_tensor(tm, tm, ps3[:, 0:tw], op=ALU.mult),
                  reads=[ws.b("tmp", c % 2), psn3], writes=[ws.b("tmp", c % 2)])
            p.add("dve", lambda E, dst=dst, tm=tm, c=c: E.tensor_scalar(
                dst, tm, gain[:, c:c + 1], bias[:, c:c + 1], op0=ALU.mult, op1=ALU.add),
                reads=[ws.b("tmp", c % 2)] + ([gbuf] if gbuf else []), writes=[dstb])



def host_l0f(inputs):
    out = {}
    for ri, (g1k, g2k) in enumerate((("L0_ffn_G1_R", "L0_ffn_G2_R"),
                                     ("L0_ffn_G1_I", "L0_ffn_G2_I"))):
        A = np.asarray(inputs[g1k], np.float32).reshape(32, 4, 32)
        B = np.asarray(inputs[g2k], np.float32).reshape(4, 16, 16)
        KA = np.zeros((4, 512, 512), np.float32)
        KB = np.zeros((4, 512, 512), np.float32)
        de = np.arange(512)
        d_, e_ = de // 16, de % 16
        for c in range(4):
            KA[c, de[:, None], (np.arange(32)[None, :] * 16 + e_[:, None])] = A[:, c, :].T[d_, :]
            KB[c, de[:, None], (d_[:, None] * 16 + np.arange(16)[None, :])] = B[c][e_, :]
        out[f"l0f_KA{ri}"] = KA
        out[f"l0f_KB{ri}"] = KB
    bcat = np.concatenate([np.asarray(inputs["L0_ffn_bias_real"], np.float32),
                           np.asarray(inputs["L0_ffn_bias_imag"], np.float32)])
    out["l0f_bias"] = np.ascontiguousarray(bcat.reshape(8, 128).T)
    return out


def emit_l0f(ctx, x_in, x_out, blk):
    p = ctx.p
    ka_d = [ctx.win(f"l0f_KA{ri}") for ri in range(2)]
    kb_d = [ctx.win(f"l0f_KB{ri}") for ri in range(2)]
    bias_d = ctx.win("l0f_bias")

    ka = ctx.sbuf(f"{blk}_ka", [128, 2 * 4 * 4 * 512], BF16)
    kb = ctx.sbuf(f"{blk}_kb", [128, 2 * 4 * 4 * 512], BF16)
    items = []
    for ri in range(2):
        for c in range(4):
            for ch in range(4):
                off = ((ri * 4 + c) * 4 + ch) * 512
                items.append((ka[:, off:off + 512], ka_d[ri][c, ch * 128:(ch + 1) * 128, :], {}))
                items.append((kb[:, off:off + 512], kb_d[ri][c, ch * 128:(ch + 1) * 128, :], {}))
    p.dma_group("pool", items, writes=[f"{blk}_ka", f"{blk}_kb"])
    wt = ctx.sbuf(f"{blk}_wt", [128, 3 * 4 * 512], BF16)
    for ri in range(2):
        for mch in range(4):
            psw, pswn = ctx.psum()

            def wmm(E, ri=ri, mch=mch, psw=psw):
                last = None
                for c in range(4):
                    for ch in range(4):
                        off = ((ri * 4 + c) * 4 + ch) * 512
                        last = E.matmul(psw[:, 0:512],
                                        kb[:, off + mch * 128: off + mch * 128 + 128],
                                        ka[:, off:off + 512],
                                        start=(c == 0 and ch == 0), stop=(c == 3 and ch == 3))
                return last
            p.add("pe", wmm, reads=[f"{blk}_ka", f"{blk}_kb"], writes=[pswn])
            woff = (ri * 4 + mch) * 512
            p.add("dve", lambda E, woff=woff, psw=psw: E.tensor_copy(wt[:, woff:woff + 512], psw[:, 0:512]),
                  reads=[pswn], writes=[f"{blk}_wt"])
            if ri == 1:
                noff = (8 + mch) * 512
                p.add("act", lambda E, noff=noff, psw=psw: E.activation(wt[:, noff:noff + 512],
                                                               psw[:, 0:512], AF.Copy, scale=-1.0),
                      reads=[pswn], writes=[f"{blk}_wt"])
    biacol = ctx.sbuf(f"{blk}_biacol", [128, 8], F32)
    p.dma("sp", biacol[:], bias_d, writes=[f"{blk}_biacol"])

    wss = [WS(ctx, f"{blk}_w{par}", 512) for par in range(2)]
    widx = 0
    import os
    maxw = int(os.environ.get("DBG_MAXW", "999"))
    for item in range(2):
        for t0 in range(0, T, 512):
            if widx >= maxw:
                continue
            ws = wss[widx % 2]
            widx += 1
            load_x_window(ctx, ws, x_in[item], t0)
            emit_ln(ctx, ws)
            for oc in range(NCH):
                po, pon = ctx.psum()
                half, mm = divmod(oc, 4)

                def fwd(E, half=half, mm=mm, po=po, ws=ws):
                    last = None
                    for kc in range(4):
                        w_r = wt[:, (0 + kc) * 512 + mm * 128:(0 + kc) * 512 + mm * 128 + 128]
                        w_i = wt[:, (4 + kc) * 512 + mm * 128:(4 + kc) * 512 + mm * 128 + 128]
                        w_in = wt[:, (8 + kc) * 512 + mm * 128:(8 + kc) * 512 + mm * 128 + 128]
                        xr = ws.xh[:, kc * 512:(kc + 1) * 512]
                        xim = ws.xh[:, (4 + kc) * 512:(4 + kc) * 512 + 512]
                        if half == 0:
                            E.matmul(po[:, 0:512], w_r, xr, start=(kc == 0), stop=False)
                            last = E.matmul(po[:, 0:512], w_in, xim, start=False, stop=(kc == 3))
                        else:
                            E.matmul(po[:, 0:512], w_i, xr, start=(kc == 0), stop=False)
                            last = E.matmul(po[:, 0:512], w_r, xim, start=False, stop=(kc == 3))
                    return last
                p.add("pe", fwd,
                      reads=[f"{blk}_wt"] + [ws.b("xh", c) for c in range(NCH)], writes=[pon])

                p.add("dve", lambda E, oc=oc, po=po: E.tensor_scalar(
                    po[:, 0:512], po[:, 0:512], biacol[:, oc:oc + 1], None, op0=ALU.add),
                    reads=[pon, f"{blk}_biacol"], writes=[pon])
                p.add("dve", lambda E, oc=oc, po=po, ws=ws: E.tensor_tensor(
                    ws.x[:, oc * 512:(oc + 1) * 512], ws.x[:, oc * 512:(oc + 1) * 512],
                    po[:, 0:512], op=ALU.add),
                    reads=[pon, ws.b("x", oc)], writes=[ws.b("x", oc)])
            store_x_window(ctx, ws, x_out[item], t0)





def host_l3a(inputs):
    out = {}
    for nm in ("Wq", "Wk", "Wv", "Wo"):
        W = np.asarray(inputs[f"L3_attn_{nm}"], np.float32)
        out[f"l3a_{nm.lower()}t"] = np.ascontiguousarray(W.T)
    for nm in ("bq", "bk", "bo"):
        b = np.asarray(inputs[f"L3_attn_{nm}"], np.float32)
        out[f"l3a_{nm}"] = np.ascontiguousarray(b.reshape(8, 128).T)
    out["l3a_bv"] = np.asarray(inputs["L3_attn_bv"], np.float32).reshape(1, 1024)
    out["l3a_mask"] = np.triu(np.full((128, 128), -30000.0, np.float32), 1)
    return out


def host_l0a(inputs):
    out = {}
    for nm in ("Wq", "Wk", "Wv"):
        W = np.asarray(inputs[f"L0_attn_{nm}"], np.float32)
        out[f"l0a_{nm.lower()}t"] = np.ascontiguousarray(
            W.transpose(1, 0, 2).reshape(1024, 1024))
    out["l0a_wot"] = np.ascontiguousarray(np.asarray(inputs["L0_attn_Wp"], np.float32))
    out["l0a_bo"] = np.ascontiguousarray(
        np.asarray(inputs["L0_attn_bp"], np.float32).reshape(8, 128).T)
    out["l0a_mask"] = np.triu(np.full((128, 128), -30000.0, np.float32), 1)
    relt = np.asarray(inputs["L0_attn_rel_table"], np.float32)
    i = np.arange(128)[:, None]
    j = np.arange(128)[None, :]
    idx1 = np.clip(16 - (i - j), 0, 32)
    d1 = relt[idx1]
    out["l0a_d1raw"] = np.ascontiguousarray(d1.transpose(2, 0, 1).reshape(16, 128, 128)
                                            .transpose(1, 0, 2).reshape(128, 16 * 128))
    jj = np.arange(16)[None, :]
    idx0 = np.clip(16 - (i + 16 - jj), 0, 32)
    d0 = relt[idx0]
    out["l0a_d0raw"] = np.ascontiguousarray(d0.transpose(2, 0, 1).reshape(16, 128, 16)
                                            .transpose(1, 0, 2).reshape(128, 16 * 16))
    out["l0a_t0rep1"] = np.ascontiguousarray(np.repeat(relt[0, :], 128).reshape(1, 2048))
    out["l0a_t0rep0"] = np.ascontiguousarray(np.repeat(relt[0, :], 16).reshape(1, 256))
    return out


def emit_attn(ctx, x_in, x_out, blk, kind):
    p = ctx.p
    pre = kind
    wq_d = ctx.win(f"{pre}_wqt")
    wk_d = ctx.win(f"{pre}_wkt")
    wv_d = ctx.win(f"{pre}_wvt")
    wo_d = ctx.win(f"{pre}_wot")
    mask_d = ctx.win(f"{pre}_mask")
    has_qkvb = kind == "l3a"
    if has_qkvb:
        bq_d, bk_d, bo_d = ctx.win("l3a_bq"), ctx.win("l3a_bk"), ctx.win("l3a_bo")
        bv_d = ctx.win("l3a_bv")
    else:
        bo_d = ctx.win("l0a_bo")
        d1_d, d0_d = ctx.win("l0a_d1raw"), ctx.win("l0a_d0raw")
        t01_d, t00_d = ctx.win("l0a_t0rep1"), ctx.win("l0a_t0rep0")

    wkv = ctx.sbuf("att_w2", [128, 2 * 8192], BF16)
    wqo = wkv
    xhi = ctx.sbuf("att_xhi", [128, NCH * T], BF16)
    kfm = ctx.sbuf("att_k", [128, 8 * T], BF16)
    vaug = ctx.sbuf("att_v", [128, 16 * 1040], BF16)
    qfm = ctx.sbuf("att_q", [128, 8 * 512], BF16)
    oall = ctx.sbuf("att_o", [128, 8 * 512], BF16)
    Psb = ctx.sbuf("att_P", [128, T], BF16)
    PT = ctx.sbuf("att_PT", [128, 2 * 512], BF16)
    mcol = ctx.sbuf("att_m", [128, 8], F32)
    avsb = ctx.sbuf("att_av", [64, 2 * 128], F32)
    rsb = ctx.sbuf("att_r", [1, 2 * 128], F32)
    maskc = ctx.sbuf("att_mask", [128, 128], F32)
    bcols = ctx.sbuf("att_bc", [128, 32], F32)
    if has_qkvb:
        bvb = ctx.sbuf("att_bvb", [128, 1024], F32)
    else:
        delta1 = ctx.sbuf("att_d1", [128, 16 * 128], F32)
        delta0 = ctx.sbuf("att_d0", [128, 16 * 16], F32)
    wss = [WS(ctx, "att_ws0", 512, need_xh=False)]

    p.dma("sp", maskc[:], mask_d, writes=["att_mask"])
    if has_qkvb:
        p.dma_group("sp", [(bcols[:, 0:8], bq_d, {}), (bcols[:, 8:16], bk_d, {}),
                           (bcols[:, 16:24], bo_d, {})], writes=["att_bc"])
        psv, psvn = ctx.psum()
        bvrow = ctx.sbuf("att_bvrow", [1, 1024], F32)
        p.dma("sp", bvrow[:], bv_d, writes=["att_bvrow"])
        for nb in range(2):
            p.add("pe", lambda E, nb=nb, psv=psv: E.matmul(
                psv[:, 0:512], ctx.onesrow[0:1, :], bvrow[0:1, nb * 512:(nb + 1) * 512],
                start=True, stop=True), reads=["att_bvrow", "c_onesrow"], writes=[psvn])
            p.add("dve", lambda E, nb=nb, psv=psv: E.tensor_copy(
                bvb[:, nb * 512:(nb + 1) * 512], psv[:, 0:512]),
                reads=[psvn], writes=["att_bvb"])
    else:
        p.dma("sp", bcols[:, 16:24], bo_d, writes=["att_bc"])
        p.dma_group("sp", [(delta1[:], d1_d, {}), (delta0[:], d0_d, {})],
                    writes=["att_d1", "att_d0"])
        t0r = ctx.sbuf("att_t0r", [1, 2048], F32)
        p.dma_group("sp", [(t0r[:, 0:2048], t01_d, {})], writes=["att_t0r"])
        t0r0 = ctx.sbuf("att_t0r0", [1, 256], F32)
        p.dma("sp", t0r0[:], t00_d, writes=["att_t0r0"])
        for seg in range(4):
            psd, psdn = ctx.psum()
            p.add("pe", lambda E, seg=seg, psd=psd: E.matmul(
                psd[:, 0:512], ctx.onesrow[0:1, :], t0r[0:1, seg * 512:(seg + 1) * 512],
                start=True, stop=True), reads=["att_t0r", "c_onesrow"], writes=[psdn])
            p.add("dve", lambda E, seg=seg, psd=psd: E.tensor_tensor(
                delta1[:, seg * 512:(seg + 1) * 512], delta1[:, seg * 512:(seg + 1) * 512],
                psd[:, 0:512], op=ALU.subtract), reads=[psdn, "att_d1"], writes=["att_d1"])
            p.add("dve", lambda E, seg=seg: E.tensor_scalar_mul(
                delta1[:, seg * 512:(seg + 1) * 512], delta1[:, seg * 512:(seg + 1) * 512], 8.0),
                reads=["att_d1"], writes=["att_d1"])
        psd, psdn = ctx.psum()
        p.add("pe", lambda E, psd=psd: E.matmul(psd[:, 0:256], ctx.onesrow[0:1, :],
                                                t0r0[0:1, 0:256], start=True, stop=True),
              reads=["att_t0r0", "c_onesrow"], writes=[psdn])
        p.add("dve", lambda E, psd=psd: E.tensor_tensor(delta0[:], delta0[:], psd[:, 0:256],
                                                        op=ALU.subtract),
              reads=[psdn, "att_d0"], writes=["att_d0"])
        p.add("dve", lambda E: E.tensor_scalar_mul(delta0[:], delta0[:], 8.0),
              reads=["att_d0"], writes=["att_d0"])

    for item in range(2):
        it = f"{blk}i{item}"
        for w in range(4):
            ws = wss[0]
            t0 = w * 512
            load_x_window(ctx, ws, x_in[item], t0)
            emit_ln(ctx, ws,
                    out_aps=[xhi[:, c * T + t0: c * T + t0 + 512] for c in range(NCH)],
                    out_bufs=[f"{it}xh{c}w{w}" for c in range(NCH)])
        xhb = lambda c, tt: f"{it}xh{c}w{tt // 4}"
        p.dma_group("pool", [(wkv[:, ch * 1024:(ch + 1) * 1024],
                              wk_d[ch * 128:(ch + 1) * 128, :], {}) for ch in range(NCH)]
                    + [(wkv[:, 8192 + ch * 1024: 8192 + (ch + 1) * 1024],
                        wv_d[ch * 128:(ch + 1) * 128, :], {}) for ch in range(NCH)],
                    writes=["att_w2"])
        for tt in range(16):
            p.add("pool", lambda E, tt=tt: E.memset(
                ap_dims(vaug[:, tt * 1040 + 64: tt * 1040 + 65], 
                        [list(vaug[:].ap[0]), [65, 16], [1, 1]]), 1.0),
                writes=[f"att_vone{tt}"])
        for tc in range(4):
            for pp in range(8):
                psk, pskn = ctx.psum()

                def kmm(E, tc=tc, pp=pp, psk=psk, item=item):
                    last = None
                    for ci in range(NCH):
                        last = E.matmul(psk[:, 0:512],
                                        wkv[:, ci * 1024 + pp * 128: ci * 1024 + pp * 128 + 128],
                                        xhi[:, ci * T + tc * 512: ci * T + (tc + 1) * 512],
                                        start=(ci == 0), stop=(ci == NCH - 1))
                    return last
                p.add("pe", kmm, reads=["att_w2"] + [xhb(c, tc * 4) for c in range(NCH)],
                      writes=[pskn])
                if has_qkvb:
                    p.add("act", lambda E, tc=tc, pp=pp, psk=psk: E.activation(
                        kfm[:, pp * T + tc * 512: pp * T + (tc + 1) * 512], psk[:, 0:512],
                        AF.Identity, bias=bcols[:, 8 + pp:9 + pp]),
                        reads=[pskn, "att_bc"], writes=[f"att_k{pp}_{tc}"])
                else:
                    p.add("act", lambda E, tc=tc, pp=pp, psk=psk: E.activation(
                        kfm[:, pp * T + tc * 512: pp * T + (tc + 1) * 512], psk[:, 0:512],
                        AF.Copy), reads=[pskn], writes=[f"att_k{pp}_{tc}"])
        for tt in range(16):
            for nb in range(2):
                psv2, psv2n = ctx.psum()

                def vmm(E, tt=tt, nb=nb, psv2=psv2):
                    last = None
                    for ci in range(NCH):
                        last = E.matmul(psv2[:, 0:512],
                                        xhi[:, ci * T + tt * 128: ci * T + (tt + 1) * 128],
                                        wkv[:, 8192 + ci * 1024 + nb * 512: 8192 + ci * 1024 + (nb + 1) * 512],
                                        start=(ci == 0), stop=(ci == NCH - 1))
                    return last
                p.add("pe", vmm, reads=["att_w2"] + [xhb(c, tt) for c in range(NCH)],
                      writes=[psv2n])
                for hh in range(8):
                    h = nb * 8 + hh
                    dst = vaug[:, tt * 1040 + h * 65: tt * 1040 + h * 65 + 64]
                    if has_qkvb:
                        p.add("dve", lambda E, dst=dst, hh=hh, h=h, psv2=psv2: E.tensor_tensor(
                            dst, psv2[:, hh * 64:(hh + 1) * 64],
                            bvb[:, h * 64:(h + 1) * 64], op=ALU.add),
                            reads=[psv2n, "att_bvb"], writes=[f"att_vt{tt}h{h}"])
                    else:
                        p.add("dve", lambda E, dst=dst, hh=hh, psv2=psv2: E.tensor_copy(
                            dst, psv2[:, hh * 64:(hh + 1) * 64]),
                            reads=[psv2n], writes=[f"att_vt{tt}h{h}"])
        p.dma_group("pool", [(wqo[:, ch * 1024:(ch + 1) * 1024],
                              wq_d[ch * 128:(ch + 1) * 128, :], {}) for ch in range(NCH)]
                    + [(wqo[:, 8192 + ch * 1024: 8192 + (ch + 1) * 1024],
                        wo_d[ch * 128:(ch + 1) * 128, :], {}) for ch in range(NCH)],
                    writes=["att_w2"])
        for tqc in range(4):
            for pp in range(8):
                psq, psqn2 = ctx.psum()

                def qmm(E, tqc=tqc, pp=pp, psq=psq):
                    last = None
                    for ci in range(NCH):
                        last = E.matmul(psq[:, 0:512],
                                        wqo[:, ci * 1024 + pp * 128: ci * 1024 + pp * 128 + 128],
                                        xhi[:, ci * T + tqc * 512: ci * T + (tqc + 1) * 512],
                                        start=(ci == 0), stop=(ci == NCH - 1))
                    return last
                p.add("pe", qmm, reads=["att_w2"] + [xhb(c, tqc * 4) for c in range(NCH)],
                      writes=[psqn2])
                if has_qkvb:
                    p.add("act", lambda E, pp=pp, psq=psq: E.activation(
                        qfm[:, pp * 512:(pp + 1) * 512], psq[:, 0:512], AF.Identity,
                        bias=bcols[:, pp:pp + 1]), reads=[psqn2, "att_bc"],
                        writes=[f"att_q{pp}"])
                else:
                    p.add("act", lambda E, pp=pp, psq=psq: E.activation(
                        qfm[:, pp * 512:(pp + 1) * 512], psq[:, 0:512], AF.Copy),
                        reads=[psqn2], writes=[f"att_q{pp}"])
            for h in range(16):
                pp, sub = divmod(h, 2)
                for t4 in range(4):
                    tqt = tqc * 4 + t4
                    nch_k = tqt // 4 + 1
                    qsl = qfm[sub * 64: sub * 64 + 64, pp * 512 + t4 * 128: pp * 512 + (t4 + 1) * 128]
                    scb = []
                    for kc in range(nch_k):
                        psx = ctx.ps[kc]
                        psxn = f"ps{kc}"
                        scb.append((psx, psxn))
                        p.add("pe", lambda E, qsl=qsl, kc=kc, pp=pp, sub=sub, psx=psx: E.matmul(
                            psx[:, 0:512], qsl,
                            kfm[sub * 64: sub * 64 + 64, pp * T + kc * 512: pp * T + (kc + 1) * 512],
                            start=True, stop=True),
                            reads=[f"att_q{pp}", f"att_k{pp}_{kc}"], writes=[psxn])
                    if kind == "l0a":
                        kcd, off = divmod(tqt * 128, 512)
                        psx = scb[kcd][0]
                        p.add("dve", lambda E, psx=psx, off=off, h=h: E.tensor_tensor(
                            psx[:, off:off + 128], psx[:, off:off + 128],
                            delta1[:, h * 128:(h + 1) * 128], op=ALU.add),
                            reads=["att_d1", scb[kcd][1]], writes=[scb[kcd][1]])
                        if tqt > 0:
                            if off >= 16:
                                psy, o2 = scb[kcd][0], off - 16
                                bn = scb[kcd][1]
                            else:
                                psy, o2 = scb[kcd - 1][0], 496
                                bn = scb[kcd - 1][1]
                            p.add("dve", lambda E, psy=psy, o2=o2, h=h: E.tensor_tensor(
                                psy[:, o2:o2 + 16], psy[:, o2:o2 + 16],
                                delta0[:, h * 16:(h + 1) * 16], op=ALU.add),
                                reads=["att_d0", bn], writes=[bn])
                    kcd, off = divmod(tqt * 128, 512)
                    psx = scb[kcd][0]
                    p.add("dve", lambda E, psx=psx, off=off: E.tensor_tensor(
                        psx[:, off:off + 128], psx[:, off:off + 128], maskc[:], op=ALU.add),
                        reads=["att_mask", scb[kcd][1]], writes=[scb[kcd][1]])
                    mc = mcol[:, t4 * 2: t4 * 2 + 1]
                    mt = mcol[:, t4 * 2 + 1: t4 * 2 + 2]
                    for kc in range(nch_k):
                        w_ = min(512, tqt * 128 + 128 - kc * 512)
                        if kc == 0:
                            p.add("dve", lambda E, kc=kc, w_=w_, mc=mc: E.reduce_max(
                                mc, scb[kc][0][:, 0:w_], axis=AX.X),
                                reads=[scb[kc][1]], writes=[f"att_mc{t4}"])
                        else:
                            p.add("dve", lambda E, kc=kc, w_=w_, mt=mt: E.reduce_max(
                                mt, scb[kc][0][:, 0:w_], axis=AX.X),
                                reads=[scb[kc][1]], writes=[f"att_mt{t4}"])
                            p.add("dve", lambda E, mc=mc, mt=mt: E.tensor_max(mc, mc, mt),
                                  reads=[f"att_mt{t4}", f"att_mc{t4}"], writes=[f"att_mc{t4}"])
                    p.add("dve", lambda E, mc=mc: E.tensor_scalar_mul(mc, mc, -0.125),
                          reads=[f"att_mc{t4}"], writes=[f"att_mc{t4}"])
                    for kc in range(nch_k):
                        w_ = min(512, tqt * 128 + 128 - kc * 512)
                        p.add("act", lambda E, kc=kc, w_=w_, mc=mc: E.activation(
                            Psb[:, kc * 512: kc * 512 + w_], scb[kc][0][:, 0:w_], AF.Exp,
                            bias=mc, scale=0.125),
                            reads=[scb[kc][1], f"att_mc{t4}"], writes=[f"att_Pk{kc}"])
                    pso = ctx.ps[4]
                    pson = "ps4"
                    ntk = tqt + 1
                    for g in range(0, ntk, 4):
                        gn = min(4, ntk - g)
                        for u in range(gn):
                            p.add("pe", lambda E, g=g, u=u: E.transpose(
                                ctx.psb[:, u * 128:(u + 1) * 128],
                                Psb[:, (g + u) * 128:(g + u + 1) * 128], ctx.ident[:]),
                                reads=[f"att_Pk{(g + u) // 4}", "c_ident"], writes=["psb"])
                        p.add("dve", lambda E, gn=gn, g=g: E.tensor_copy(
                            PT[:, (g // 4 % 2) * 512: (g // 4 % 2) * 512 + gn * 128],
                            ctx.psb[:, 0:gn * 128]), reads=["psb"], writes=[f"att_PT{g // 4 % 2}"])
                        for u in range(gn):
                            tkt = g + u
                            p.add("pe", lambda E, tkt=tkt, u=u, g=g, h=h, pso=pso,
                                  first=(tkt == 0), last=(tkt == ntk - 1): E.matmul(
                                pso[0:65, 0:128],
                                vaug[:, tkt * 1040 + h * 65: tkt * 1040 + h * 65 + 65],
                                PT[:, (g // 4 % 2) * 512 + u * 128: (g // 4 % 2) * 512 + (u + 1) * 128],
                                start=first, stop=last),
                                reads=[f"att_PT{g // 4 % 2}", f"att_vt{tkt}h{h}", f"att_vone{tkt}"],
                                writes=[pson], inc=(1 if tkt == ntk - 1 else 0))
                    rs = rsb[0:1, (t4 % 2) * 128: (t4 % 2) * 128 + 128]
                    p.add("dve", lambda E, rs=rs, pso=pso: E.reciprocal(rs, pso[64:65, 0:128]),
                          reads=[pson], writes=[f"att_r{t4 % 2}"])
                    av = avsb[:, (t4 % 2) * 128: (t4 % 2) * 128 + 128]
                    p.add("dve", lambda E, av=av, pso=pso: E.tensor_copy(av, pso[0:64, 0:128]),
                          reads=[pson], writes=[f"att_av{t4 % 2}"])
                    psz = ctx.ps[5]
                    pszn = "ps5"
                    p.add("pe", lambda E, rs=rs, psz=psz: E.matmul(
                        psz[0:64, 0:128], ctx.onesrow[0:1, 0:64], rs, start=True, stop=True),
                        reads=[f"att_r{t4 % 2}", "c_onesrow"], writes=[pszn])
                    od = oall[sub * 64: sub * 64 + 64, pp * 512 + t4 * 128: pp * 512 + (t4 + 1) * 128]
                    p.add("dve", lambda E, od=od, av=av, psz=psz: E.tensor_tensor(
                        od, av, psz[0:64, 0:128], op=ALU.mult),
                        reads=[f"att_av{t4 % 2}", pszn], writes=[f"att_o{pp}s{sub}"])
            ws = wss[0]
            load_x_window(ctx, ws, x_in[item], tqc * 512)
            for oc in range(NCH):
                pso2, pso2n = ctx.psum()

                def pmm(E, oc=oc, pso2=pso2):
                    last = None
                    for ci in range(NCH):
                        last = E.matmul(pso2[:, 0:512],
                                        wqo[:, 8192 + ci * 1024 + oc * 128: 8192 + ci * 1024 + oc * 128 + 128],
                                        oall[:, ci * 512:(ci + 1) * 512],
                                        start=(ci == 0), stop=(ci == NCH - 1))
                    return last
                p.add("pe", pmm, reads=["att_w2"] + [f"att_o{cc}s{ss}" for cc in range(NCH)
                                                      for ss in range(2)], writes=[pso2n])
                p.add("dve", lambda E, oc=oc, pso2=pso2: E.tensor_scalar(
                    pso2[:, 0:512], pso2[:, 0:512], bcols[:, 16 + oc:17 + oc], None, op0=ALU.add),
                    reads=[pso2n, "att_bc"], writes=[pso2n])
                p.add("dve", lambda E, oc=oc, ws=ws, pso2=pso2: E.tensor_tensor(
                    ws.x[:, oc * 512:(oc + 1) * 512], ws.x[:, oc * 512:(oc + 1) * 512],
                    pso2[:, 0:512], op=ALU.add),
                    reads=[pso2n, ws.b("x", oc)], writes=[ws.b("x", oc)])
            store_x_window(ctx, ws, x_out[item], tqc * 512)


def emit_l3a(ctx, x_in, x_out, blk):
    emit_attn(ctx, x_in, x_out, blk, "l3a")


def emit_l0a(ctx, x_in, x_out, blk):
    emit_attn(ctx, x_in, x_out, blk, "l0a")




def host_l1f(inputs):
    mk = np.asarray(inputs["L1_ffn_rail_maker"], np.float32)
    up = np.asarray(inputs["L1_ffn_rail_up_proj"], np.float32)
    mkT_bd = np.zeros((4, 8, 128, 128), np.float32)
    upT_bd = np.zeros((4, 8, 128, 512), np.float32)
    for h in range(4):
        for g in range(8):
            for rr in range(8):
                r = g * 8 + rr
                mkT_bd[h, g, rr * 16:(rr + 1) * 16, rr * 16:(rr + 1) * 16] = mk[h, r].T
                upT_bd[h, g, rr * 16:(rr + 1) * 16, rr * 64:(rr + 1) * 64] = up[h, r].T
    wc = np.asarray(inputs["L1_ffn_w_cproj"], np.float32)
    return {"l1f_mkbd": mkT_bd.reshape(32, 128, 128),
            "l1f_upbd": upT_bd.reshape(32, 128, 512),
            "l1f_wct": np.ascontiguousarray(wc.T)}


def emit_l1f(ctx, x_in, x_out, blk):
    p = ctx.p
    mk_d = ctx.win("l1f_mkbd")
    up_d = ctx.win("l1f_upbd")
    wc_d = ctx.win("l1f_wct")
    bd = ctx.sbuf("l1f_bd", [128, 32 * 512], BF16)
    xhi = ctx.sbuf("att_xhi", [128, NCH * T], BF16)
    hsb = ctx.sbuf("l1f_h", [128, 16 * 512], BF16)
    oacc = ctx.sbuf("l1f_oacc", [128, NCH * 512], F32)
    wc2 = [ctx.sbuf(f"l1f_wc{i}", [128, 16 * 1024], BF16) for i in range(2)]
    mkb = wc2[0]
    upb = wc2[1]
    wss = [WS(ctx, "att_ws0", 512, need_xh=False)]
    p.dma_group("pool", [(mkb[:, i * 128:(i + 1) * 128], mk_d[i], {}) for i in range(32)],
                writes=["l1f_mkb"])
    p.dma_group("pool", [(upb[:, i * 512:(i + 1) * 512], up_d[i], {}) for i in range(32)],
                writes=["l1f_upb"])
    for i in range(32):
        psw, pswn = ctx.psum()
        p.add("pe", lambda E, i=i, psw=psw: E.matmul(
            psw[:, 0:512], mkb[:, i * 128:(i + 1) * 128], upb[:, i * 512:(i + 1) * 512],
            start=True, stop=True), reads=["l1f_mkb", "l1f_upb"], writes=[pswn])
        p.add("dve", lambda E, i=i, psw=psw: E.tensor_copy(bd[:, i * 512:(i + 1) * 512],
                                                           psw[:, 0:512]),
              reads=[pswn], writes=["l1f_bd"])

    for item in range(2):
        it = f"{blk}i{item}"
        for w in range(4):
            ws = wss[0]
            load_x_window(ctx, ws, x_in[item], w * 512)
            emit_ln(ctx, ws,
                    out_aps=[xhi[:, c * T + w * 512: c * T + w * 512 + 512] for c in range(NCH)],
                    out_bufs=[f"{it}xh{c}w{w}" for c in range(NCH)])
        for tc in range(4):
            for oc in range(NCH):
                p.add("dve", lambda E, oc=oc: E.memset(oacc[:, oc * 512:(oc + 1) * 512], 0.0),
                      writes=[f"l1f_oa{oc}"])
            for hb in range(8):
                h4 = hb // 2
                g0 = (hb % 2) * 4
                wcb = wc2[hb % 2]
                p.dma_group("pool", [(wcb[:, j * 1024:(j + 1) * 1024],
                                      wc_d[hb * 2048 + j * 128: hb * 2048 + (j + 1) * 128, :], {})
                                     for j in range(16)],
                            writes=[f"l1f_wcb{hb % 2}",
                                    "l1f_mkb" if hb % 2 == 0 else "l1f_upb"])
                for gg in range(4):
                    g = g0 + gg
                    bdi = h4 * 8 + g
                    for q in range(4):
                        psz, pszn = ctx.psum()
                        p.add("pe", lambda E, bdi=bdi, q=q, g=g, tc=tc, psz=psz: E.matmul(
                            psz[:, 0:512],
                            bd[:, bdi * 512 + q * 128: bdi * 512 + q * 128 + 128],
                            xhi[:, g * T + tc * 512: g * T + (tc + 1) * 512],
                            start=True, stop=True),
                            reads=["l1f_bd", f"{it}xh{g}w{tc}"], writes=[pszn])
                        hcn = gg * 4 + q
                        p.add("act", lambda E, hcn=hcn, psz=psz: E.activation(
                            hsb[:, hcn * 512:(hcn + 1) * 512], psz[:, 0:512], AF.Gelu),
                            reads=[pszn], writes=[f"l1f_h{hcn}"])
                for oc in range(NCH):
                    pso, pson = ctx.psum()

                    def cpmm(E, oc=oc, pso=pso, wcb=wcb):
                        last = None
                        for j in range(16):
                            last = E.matmul(pso[:, 0:512],
                                            wcb[:, j * 1024 + oc * 128: j * 1024 + oc * 128 + 128],
                                            hsb[:, j * 512:(j + 1) * 512],
                                            start=(j == 0), stop=(j == 15))
                        return last
                    p.add("pe", cpmm, reads=[f"l1f_wcb{hb % 2}"] + [f"l1f_h{j}" for j in range(16)],
                          writes=[pson])
                    p.add("dve", lambda E, oc=oc, pso=pso: E.tensor_tensor(
                        oacc[:, oc * 512:(oc + 1) * 512], oacc[:, oc * 512:(oc + 1) * 512],
                        pso[:, 0:512], op=ALU.add), reads=[pson, f"l1f_oa{oc}"],
                        writes=[f"l1f_oa{oc}"])
            ws = wss[0]
            load_x_window(ctx, ws, x_in[item], tc * 512)
            for oc in range(NCH):
                p.add("dve", lambda E, oc=oc, ws=ws: E.tensor_tensor(
                    ws.x[:, oc * 512:(oc + 1) * 512], ws.x[:, oc * 512:(oc + 1) * 512],
                    oacc[:, oc * 512:(oc + 1) * 512], op=ALU.add),
                    reads=[f"l1f_oa{oc}", ws.b("x", oc)], writes=[ws.b("x", oc)])
            store_x_window(ctx, ws, x_out[item], tc * 512)



def host_l2a(inputs):
    wq = np.asarray(inputs["L2_attn_w_qkvg"], np.float32)
    wp = np.asarray(inputs["L2_attn_w_proj"], np.float32)
    bq = np.asarray(inputs["L2_attn_b_qkvg"], np.float32)
    bp = np.asarray(inputs["L2_attn_b_proj"], np.float32)
    wpt2 = np.zeros((128, 16, 1024), np.float32)
    for ta in range(16):
        wpt2[0:64, ta, :] = wp[:, ta * 64:(ta + 1) * 64].T
        wpt2[64:128, ta, :] = wp[:, ta * 64:(ta + 1) * 64].T
    return {
        "l2a_wt": np.ascontiguousarray(wq.T),
        "l2a_wpt": np.ascontiguousarray(wpt2.reshape(128, 16384)),
        "l2a_bq": bq.reshape(1, 4096),
        "l2a_bvcols": np.ascontiguousarray(bq[2048:3072].reshape(8, 128).T),
        "l2a_bgcols": np.ascontiguousarray(bq[3072:4096].reshape(8, 128).T),
        "l2a_bp": np.ascontiguousarray(bp.reshape(8, 128).T),
        "l2a_identf": np.eye(64, dtype=np.float32),
    }


def emit_l2a(ctx, x_in, x_out, blk):
    import os
    DBG = int(os.environ.get("DBG_L2A", "7"))
    p = ctx.p
    wt_d = ctx.win("l2a_wt")
    wpt_d = ctx.win("l2a_wpt")
    bq_d = ctx.win("l2a_bq")
    bv_d = ctx.win("l2a_bvcols")
    bp_d = ctx.win("l2a_bp")
    idf_d = ctx.win("l2a_identf")

    xhi = ctx.sbuf("att_xhi", [128, NCH * T], BF16)
    o2t = xhi
    w2 = ctx.sbuf("l2a_w2", [128, NCH * 2048], BF16)
    wpt = ctx.sbuf("l2a_wpts", [128, 16 * 1024], BF16)
    vfm = ctx.sbuf("l2a_v", [128, 8 * 128], BF16)
    gfm = ctx.sbuf("l2a_g", [128, 8 * 512], BF16)
    qt = ctx.sbuf("l2a_qt", [128, 1024], BF16)
    kaug = ctx.sbuf("l2a_ka", [128, 16 * 65], BF16)
    aacc = ctx.sbuf("l2a_aacc", [128, 1024], F32)
    Asb = ctx.sbuf("l2a_A", [128, 8 * 64], BF16)
    zsb = ctx.sbuf("l2a_z", [1, 1024], F32)
    bcolsg = ctx.sbuf("l2a_bg", [128, 8], F32)
    qkb = ctx.sbuf("l2a_qkb", [128, 2048], BF16)
    bcolsv = ctx.sbuf("l2a_bvs", [128, 8], F32)
    bcolsp = ctx.sbuf("l2a_bps", [128, 8], F32)
    idf = ctx.sbuf("l2a_idf", [64, 64], F32)
    sp1 = ctx.sbuf("l2a_sp", [128, 2 * 512], F32)
    wss = [WS(ctx, "att_ws0", 512, need_xh=False)]
    bqrow = wss[0].x

    p.dma_group("pool", [(wpt[:, j * 1024:(j + 1) * 1024],
                          wpt_d[:, j * 1024:(j + 1) * 1024], {}) for j in range(16)],
                writes=["l2a_wptb"])
    p.dma_group("sp", [(bqrow[0:1, 0:4096], bq_d, {}), (bcolsv[:], bv_d, {}),
                       (bcolsp[:], bp_d, {}), (idf[:], idf_d, {})],
                writes=["l2a_bvb", "l2a_bpb", "l2a_idfb"] + [wss[0].b("x", c) for c in range(NCH)])
    bg_d = ctx.win("l2a_bgcols")
    p.dma("sp", bcolsg[:], bg_d, writes=["l2a_bgb"])
    for nb in range(4):
        psv, psvn = ctx.psum()
        p.add("pe", lambda E, nb=nb, psv=psv: E.matmul(
            psv[:, 0:512], ctx.onesrow[0:1, :], bqrow[0:1, nb * 512:(nb + 1) * 512],
            start=True, stop=True), reads=[wss[0].b("x", 0), "c_onesrow"], writes=[psvn])
        p.add("dve", lambda E, nb=nb, psv=psv: E.tensor_copy(qkb[:, nb * 512:(nb + 1) * 512],
                                                             psv[:, 0:512]),
              reads=[psvn], writes=["l2a_qkbb"])

    for item in range(2):
        it = f"{blk}i{item}"
        xb = lambda c, w: f"l2axh{c}_{w}"
        for w in range(4):
            ws = wss[0]
            load_x_window(ctx, ws, x_in[item], w * 512)
            emit_ln(ctx, ws,
                    out_aps=[xhi[:, c * T + w * 512: c * T + w * 512 + 512] for c in range(NCH)],
                    out_bufs=[xb(c, w) for c in range(NCH)])
        if not DBG & 1:
            continue
        p.dma_group("pool", [(w2[:, ci * 2048:(ci + 1) * 2048],
                              wt_d[ci * 128:(ci + 1) * 128, 0:2048], {}) for ci in range(NCH)],
                    writes=["l2a_wm"])
        for hf in range(2):
            p.add("dve", lambda E, hf=hf: E.memset(aacc[0:65, hf * 512:(hf + 1) * 512], 0.0),
                  writes=[f"l2a_aacc{hf}"])
        for tt in range(16):
            for half, dstb in ((0, "l2a_qtb"), (1, "l2a_kab")):
                for nb in range(2):
                    psq, psqn = ctx.psum()

                    def qkmm(E, tt=tt, half=half, nb=nb, psq=psq):
                        last = None
                        for ci in range(NCH):
                            last = E.matmul(psq[:, 0:512],
                                            xhi[:, ci * T + tt * 128: ci * T + (tt + 1) * 128],
                                            w2[:, ci * 2048 + half * 1024 + nb * 512:
                                                  ci * 2048 + half * 1024 + (nb + 1) * 512],
                                            start=(ci == 0), stop=(ci == NCH - 1))
                        return last
                    p.add("pe", qkmm, reads=["l2a_wm"] + [xb(c, tt // 4) for c in range(NCH)],
                          writes=[psqn])
                    p.add("dve", lambda E, half=half, nb=nb, psq=psq: E.tensor_tensor(
                        sp1[:, 0:512], psq[:, 0:512],
                        qkb[:, half * 1024 + nb * 512: half * 1024 + (nb + 1) * 512], op=ALU.add),
                        reads=[psqn, "l2a_qkbb"], writes=["l2a_sp0"])
                    p.add("act", lambda E: E.activation(sp1[:, 512:1024], sp1[:, 0:512], AF.Exp),
                          reads=["l2a_sp0"], writes=["l2a_sp1"])
                    p.add("dve", lambda E: E.tensor_scalar_add(sp1[:, 512:1024],
                                                               sp1[:, 512:1024], 1.0),
                          reads=["l2a_sp1"], writes=["l2a_sp1"])
                    if half == 0:
                        p.add("act", lambda E, nb=nb: E.activation(
                            qt[:, nb * 512:(nb + 1) * 512], sp1[:, 512:1024], AF.Ln),
                            reads=["l2a_sp1"], writes=[dstb])
                    else:
                        kdst = ap_dims(kaug[:, nb * 8 * 65: nb * 8 * 65 + 1],
                                       [list(kaug[:].ap[0]), [65, 8], [1, 64]])
                        p.add("act", lambda E, kdst=kdst: E.activation(kdst, sp1[:, 512:1024],
                                                                       AF.Ln),
                              reads=["l2a_sp1"], writes=[dstb])
            for h in range(16):
                p.add("dve", lambda E, h=h: E.reduce_sum(
                    sp1[:, h:h + 1], kaug[:, h * 65: h * 65 + 64],
                    axis=AX.X), reads=["l2a_kab"], writes=["l2a_sp0"])
            p.add("dve", lambda E: E.tensor_copy(
                ap_dims(kaug[:, 64:65], [list(kaug[:].ap[0]), [65, 16], [1, 1]]),
                sp1[:, 0:16]), reads=["l2a_sp0"], writes=["l2a_kab"])
            for hf in range(2):
                psA, psAn = ctx.psum()

                def amm(E, tt=tt, hf=hf, psA=psA):
                    last = None
                    for hh in range(8):
                        h = hf * 8 + hh
                        last = E.matmul(psA[0:65, hh * 64: hh * 64 + 64],
                                        kaug[:, h * 65: h * 65 + 65],
                                        qt[:, h * 64: h * 64 + 64],
                                        start=True, stop=True)
                    return last
                p.add("pe", amm, reads=["l2a_qtb", "l2a_kab"], writes=[psAn])
                p.add("dve", lambda E, hf=hf, psA=psA: E.tensor_tensor(
                    aacc[0:65, hf * 512:(hf + 1) * 512], aacc[0:65, hf * 512:(hf + 1) * 512],
                    psA[0:65, 0:512], op=ALU.add),
                    reads=[psAn, f"l2a_aacc{hf}"], writes=[f"l2a_aacc{hf}"])
        for h in range(16):
            hf, hh = divmod(h, 8)
            p.add("dve", lambda E, h=h, hf=hf, hh=hh: E.tensor_scalar(
                zsb[0:1, h * 64:(h + 1) * 64], aacc[64:65, hf * 512 + hh * 64: hf * 512 + hh * 64 + 64],
                0.125, 1024.0, op0=ALU.mult, op1=ALU.add),
                reads=[f"l2a_aacc{hf}"], writes=["l2a_zsb"])
        p.add("dve", lambda E: E.reciprocal(zsb[:], zsb[:]), reads=["l2a_zsb"],
              writes=["l2a_zsb"])
        for h in range(16):
            hf, hh = divmod(h, 8)
            pp2, sub2 = divmod(h, 2)
            psz, pszn = ctx.psum()
            p.add("pe", lambda E, h=h, psz=psz: E.matmul(
                psz[0:64, 0:64], ctx.onesrow[0:1, 0:64], zsb[0:1, h * 64:(h + 1) * 64],
                start=True, stop=True), reads=["l2a_zsb", "c_onesrow"], writes=[pszn])
            p.add("dve", lambda E, hf=hf, hh=hh: E.scalar_tensor_tensor(
                sp1[0:64, 0:64], aacc[0:64, hf * 512 + hh * 64: hf * 512 + hh * 64 + 64],
                0.125, idf[:], op0=ALU.mult, op1=ALU.add),
                reads=[f"l2a_aacc{hf}", "l2a_idfb"], writes=["l2a_sp0"])
            p.add("dve", lambda E, pp2=pp2, sub2=sub2, psz=psz: E.tensor_tensor(
                Asb[sub2 * 64: sub2 * 64 + 64, pp2 * 64:(pp2 + 1) * 64],
                sp1[0:64, 0:64], psz[0:64, 0:64], op=ALU.mult),
                reads=["l2a_sp0", pszn], writes=["l2a_Ab"])
        if not DBG & 2:
            continue
        p.dma_group("pool", [(w2[:, ci * 2048:(ci + 1) * 2048],
                              wt_d[ci * 128:(ci + 1) * 128, 2048:4096], {}) for ci in range(NCH)],
                    writes=["l2a_wm"])
        for tc in range(4):
            for pp in range(8):
                psg, psgn = ctx.psum()

                def gmm(E, tc=tc, pp=pp, psg=psg):
                    last = None
                    for ci in range(NCH):
                        last = E.matmul(psg[:, 0:512],
                                        w2[:, ci * 2048 + 1024 + pp * 128: ci * 2048 + 1024 + pp * 128 + 128],
                                        xhi[:, ci * T + tc * 512: ci * T + (tc + 1) * 512],
                                        start=(ci == 0), stop=(ci == NCH - 1))
                    return last
                p.add("pe", gmm, reads=["l2a_wm"] + [xb(c, tc) for c in range(NCH)],
                      writes=[psgn])
                p.add("act", lambda E, pp=pp, psg=psg: E.activation(
                    gfm[:, pp * 512:(pp + 1) * 512], psg[:, 0:512], AF.Gelu,
                    bias=bcolsg[:, pp:pp + 1]),
                    reads=[psgn, "l2a_bgb"], writes=[f"l2a_g{pp}"])
            for t4 in range(4):
                tt = tc * 4 + t4
                for pp in range(8):
                    psk, pskn = ctx.psum()

                    def vmm(E, pp=pp, tt=tt, psk=psk):
                        last = None
                        for ci in range(NCH):
                            last = E.matmul(psk[:, 0:128],
                                            w2[:, ci * 2048 + pp * 128: ci * 2048 + pp * 128 + 128],
                                            xhi[:, ci * T + tt * 128: ci * T + (tt + 1) * 128],
                                            start=(ci == 0), stop=(ci == NCH - 1))
                        return last
                    p.add("pe", vmm, reads=["l2a_wm"] + [xb(c, tc) for c in range(NCH)],
                          writes=[pskn])
                    p.add("act", lambda E, pp=pp, psk=psk: E.activation(
                        vfm[:, pp * 128:(pp + 1) * 128], psk[:, 0:128],
                        AF.Gelu, bias=bcolsv[:, pp:pp + 1]),
                        reads=[pskn, "l2a_bvb"], writes=[f"l2a_v{pp}"])
                for h in range(16):
                    pp, sub = divmod(h, 2)
                    pso, pson = ctx.psum()
                    p.add("pe", lambda E, pp=pp, sub=sub, pso=pso: E.matmul(
                        pso[0:64, 0:128],
                        Asb[sub * 64: sub * 64 + 64, pp * 64:(pp + 1) * 64],
                        vfm[sub * 64: sub * 64 + 64, pp * 128:(pp + 1) * 128],
                        start=True, stop=True),
                        reads=["l2a_Ab", f"l2a_v{pp}"], writes=[pson])
                    p.add("dve", lambda E, h=h, pp=pp, sub=sub, tt=tt, t4=t4, pso=pso:
                          E.tensor_tensor(
                        o2t[sub * 64: sub * 64 + 64, pp * T + tt * 128: pp * T + (tt + 1) * 128],
                        pso[0:64, 0:128],
                        gfm[sub * 64: sub * 64 + 64, pp * 512 + t4 * 128: pp * 512 + (t4 + 1) * 128],
                        op=ALU.mult),
                        reads=[pson, f"l2a_g{pp}"], writes=[xb(pp, tc)])
        if not DBG & 4:
            continue
        for wv in range(4):
            ws = wss[0]
            load_x_window(ctx, ws, x_in[item], wv * 512)
            for hcol in range(4):
                h = wv * 4 + hcol
                pp, sub = divmod(h, 2)
                for oc in range(NCH):
                    psp, pspn = ctx.psum()

                    def smm(E, h=h, pp=pp, sub=sub, oc=oc, psp=psp):
                        last = None
                        for ta in range(16):
                            rhs = ap_dims(
                                o2t[sub * 64: sub * 64 + 64, pp * T + ta: pp * T + ta + 1],
                                [list(o2t[sub * 64: sub * 64 + 64, 0:1].ap[0]), [16, 128]])
                            last = E.matmul(psp[:, 0:128],
                                            wpt[sub * 64: sub * 64 + 64,
                                                ta * 1024 + oc * 128: ta * 1024 + oc * 128 + 128],
                                            rhs, start=(ta == 0), stop=(ta == 15))
                        return last
                    p.add("pe", smm, reads=["l2a_wptb"] + [xb(pp, j) for j in range(4)],
                          writes=[pspn])
                    p.add("dve", lambda E, oc=oc, psp=psp: E.tensor_scalar(
                        psp[:, 0:128], psp[:, 0:128], bcolsp[:, oc:oc + 1], None, op0=ALU.add),
                        reads=[pspn, "l2a_bpb"], writes=[pspn])
                    p.add("dve", lambda E, oc=oc, hcol=hcol, ws=ws, psp=psp: E.tensor_tensor(
                        ws.x[:, oc * 512 + hcol * 128: oc * 512 + hcol * 128 + 128],
                        ws.x[:, oc * 512 + hcol * 128: oc * 512 + hcol * 128 + 128],
                        psp[:, 0:128], op=ALU.add),
                        reads=[pspn, ws.b("x", oc)], writes=[ws.b("x", oc)])
            store_x_window(ctx, ws, x_out[item], wv * 512)



def _perm_gu(F2):
    ev = np.arange(0, F2, 2)
    od = np.arange(1, F2, 2)
    out = []
    for p in range(F2 // 256):
        out.append(ev[p * 128:(p + 1) * 128])
        out.append(od[p * 128:(p + 1) * 128])
    return np.concatenate(out)


def host_l2f(inputs):
    P = _perm_gu(2048)
    w1 = np.asarray(inputs["L2_ffn_w1"], np.float32)[:, P, :]
    w3 = np.asarray(inputs["L2_ffn_w3"], np.float32)[:, P, :]
    b1 = np.asarray(inputs["L2_ffn_b1"], np.float32)[:, P]
    b3 = np.asarray(inputs["L2_ffn_b3"], np.float32)[:, P]
    w2 = np.asarray(inputs["L2_ffn_w2"], np.float32)
    b2 = np.asarray(inputs["L2_ffn_b2"], np.float32)
    sw1 = np.asarray(inputs["L2_ffn_sw1"], np.float32)[P, :]
    sw3 = np.asarray(inputs["L2_ffn_sw3"], np.float32)[P, :]
    sb1 = np.asarray(inputs["L2_ffn_sb1"], np.float32)[P]
    sb3 = np.asarray(inputs["L2_ffn_sb3"], np.float32)[P]
    sw2 = np.asarray(inputs["L2_ffn_sw2"], np.float32)
    sb2 = np.asarray(inputs["L2_ffn_sb2"], np.float32)
    w1t = np.concatenate([w1.transpose(0, 2, 1), sw1.T[None]], 0)
    w3t = np.concatenate([w3.transpose(0, 2, 1), sw3.T[None]], 0)
    w2t = np.concatenate([w2.transpose(0, 2, 1), sw2.T[None]], 0)
    b1c = np.concatenate([b1, sb1[None]], 0).reshape(9, 16, 128).transpose(2, 0, 1)
    b3c = np.concatenate([b3, sb3[None]], 0).reshape(9, 16, 128).transpose(2, 0, 1)
    b2c = np.concatenate([b2, sb2[None]], 0).reshape(9, 8, 128).transpose(2, 0, 1)
    sel = np.zeros((8, 8 * 128), np.float32)
    for e in range(8):
        sel[e, e * 128:(e + 1) * 128] = 1.0
    return {
        "l2f_w1t": np.ascontiguousarray(w1t), "l2f_w3t": np.ascontiguousarray(w3t),
        "l2f_w2t": np.ascontiguousarray(w2t),
        "l2f_b1c": np.ascontiguousarray(b1c.reshape(128, 144)),
        "l2f_b3c": np.ascontiguousarray(b3c.reshape(128, 144)),
        "l2f_b2c": np.ascontiguousarray(b2c.reshape(128, 72)),
        "l2f_gwt": np.ascontiguousarray(np.asarray(inputs["L2_ffn_gate_w"], np.float32).T),
        "l2f_gbr": np.asarray(inputs["L2_ffn_gate_b"], np.float32).reshape(1, 8),
        "l2f_sel": sel,
        "l2f_id128f": np.eye(128, dtype=np.float32),
    }


def emit_swiglu(ctx, p, gt_ap, lt_ap, out_ap, tagg, tagl, tago, sc1, sc2, tw=512):
    p.add("dve", lambda E: E.tensor_scalar_min(sc1[:, 0:tw], gt_ap, 7.0),
          reads=[tagg], writes=["swg_s1"])
    p.add("act", lambda E: E.activation(sc2[:, 0:tw], sc1[:, 0:tw], AF.Sigmoid, scale=1.702),
          reads=["swg_s1"], writes=["swg_s2"])
    p.add("dve", lambda E: E.tensor_tensor(sc1[:, 0:tw], sc1[:, 0:tw], sc2[:, 0:tw],
                                           op=ALU.mult), reads=["swg_s1", "swg_s2"],
          writes=["swg_s1"])
    p.add("dve", lambda E: E.tensor_scalar(sc2[:, 0:tw], lt_ap, -7.0, 7.0,
                                           op0=ALU.max, op1=ALU.min),
          reads=[tagl], writes=["swg_s2"])
    p.add("dve", lambda E: E.tensor_scalar_add(sc2[:, 0:tw], sc2[:, 0:tw], 1.0),
          reads=["swg_s2"], writes=["swg_s2"])
    p.add("dve", lambda E: E.tensor_tensor(out_ap, sc1[:, 0:tw], sc2[:, 0:tw], op=ALU.mult),
          reads=["swg_s1", "swg_s2"], writes=[tago])


def emit_l2f(ctx, x_in, x_out, blk):
    p = ctx.p
    w1_d = ctx.win("l2f_w1t")
    w3_d = ctx.win("l2f_w3t")
    w2_d = ctx.win("l2f_w2t")
    b1_d = ctx.win("l2f_b1c")
    b3_d = ctx.win("l2f_b3c")
    b2_d = ctx.win("l2f_b2c")
    gw_d = ctx.win("l2f_gwt")
    gb_d = ctx.win("l2f_gbr")
    sel_d = ctx.win("l2f_sel")
    idf_d = ctx.win("l2f_id128f")

    xhi = ctx.sbuf("att_xhi", [128, NCH * T], BF16)
    racc = ctx.sbuf("l2f_racc", [128, NCH * 1024], F32)
    w13 = [ctx.sbuf(f"l2f_w13_{i}", [128, 2 * 2048], BF16) for i in range(2)]
    w2sb = ctx.sbuf("l2f_w2sb", [128, 8 * 1024], BF16)
    gwsb = ctx.sbuf("l2f_gws", [128, 64], BF16)
    hsb = ctx.sbuf("l2f_hsb", [128, 2 * 512], F32)
    asb = ctx.sbuf("l2f_asb", [128, 2 * 8 * 512], BF16)
    sc1 = ctx.sbuf("swg_sc1", [128, 512], F32)
    sc2 = ctx.sbuf("swg_sc2", [128, 512], F32)
    bcols = ctx.sbuf("l2f_bc", [128, 144 * 2 + 72], F32)
    cwt = ctx.sbuf("l2f_cwts", [128, 16 * 8], F32)
    cwT = ctx.sbuf("l2f_cwT", [8, T], F32)
    cwb = ctx.sbuf("l2f_cwb", [128, 512], F32)
    selsb = ctx.sbuf("l2f_sels", [8, 8 * 128], F32)
    gbb = ctx.sbuf("l2f_gbb", [128, 8], F32)
    gbrow = ctx.sbuf("l2f_gbrs", [1, 8], F32)
    idfsb = ctx.sbuf("l2f_idfs", [128, 128], F32)
    wss = [WS(ctx, "att_ws0", 512, need_xh=False)]

    p.dma_group("sp", [(bcols[:, 0:144], b1_d, {}), (bcols[:, 144:288], b3_d, {}),
                       (bcols[:, 288:360], b2_d, {}), (selsb[:], sel_d, {}),
                       (gbrow[:], gb_d, {}), (idfsb[:], idf_d, {})],
                writes=["l2f_bcb", "l2f_selb", "l2f_gbrb", "l2f_idfb"])
    p.dma_group("pool", [(gwsb[:, ci * 8:(ci + 1) * 8], gw_d[ci * 128:(ci + 1) * 128, :], {})
                         for ci in range(NCH)], writes=["l2f_gwb"])
    psg, psgn = ctx.psum()
    p.add("pe", lambda E: E.matmul(psg[:, 0:8], ctx.onesrow[0:1, :], gbrow[0:1, :],
                                   start=True, stop=True),
          reads=["l2f_gbrb", "c_onesrow"], writes=[psgn])
    p.add("dve", lambda E: E.tensor_copy(gbb[:], psg[:, 0:8]), reads=[psgn], writes=["l2f_gbbb"])

    for item in range(2):
        it = f"{blk}i{item}"
        xb = lambda c, w: f"l2fxh{c}_{w}"
        for w in range(4):
            ws = wss[0]
            load_x_window(ctx, ws, x_in[item], w * 512)
            emit_ln(ctx, ws,
                    out_aps=[xhi[:, c * T + w * 512: c * T + w * 512 + 512] for c in range(NCH)],
                    out_bufs=[xb(c, w) for c in range(NCH)])
        for tt in range(16):
            psl, psln = ctx.psum()

            def lmm(E, tt=tt, psl=psl):
                last = None
                for ci in range(NCH):
                    last = E.matmul(psl[:, 0:8], xhi[:, ci * T + tt * 128: ci * T + (tt + 1) * 128],
                                    gwsb[:, ci * 8:(ci + 1) * 8],
                                    start=(ci == 0), stop=(ci == NCH - 1))
                return last
            p.add("pe", lmm, reads=["l2f_gwb"] + [xb(c, tt // 4) for c in range(NCH)],
                  writes=[psln])
            cwsl = cwt[:, tt * 8:(tt + 1) * 8]
            p.add("dve", lambda E, psl=psl: E.reduce_max(sc1[:, 0:1], psl[:, 0:8], axis=AX.X),
                  reads=[psln], writes=["swg_s1"])
            p.add("dve", lambda E: E.tensor_scalar_mul(sc1[:, 0:1], sc1[:, 0:1], -1.0),
                  reads=["swg_s1"], writes=["swg_s1"])
            p.add("act", lambda E, psl=psl: E.activation(sc2[:, 0:8], psl[:, 0:8], AF.Exp,
                                                         bias=sc1[:, 0:1]),
                  reads=[psln, "swg_s1"], writes=["swg_s2"])
            p.add("dve", lambda E: E.reduce_sum(sc1[:, 1:2], sc2[:, 0:8], axis=AX.X),
                  reads=["swg_s2"], writes=["swg_s1"])
            p.add("dve", lambda E: E.reciprocal(sc1[:, 1:2], sc1[:, 1:2]),
                  reads=["swg_s1"], writes=["swg_s1"])
            p.add("dve", lambda E, cwsl=cwsl: E.tensor_scalar_mul(cwsl, sc2[:, 0:8], sc1[:, 1:2]),
                  reads=["swg_s1", "swg_s2"], writes=["l2f_cwb1"])
            p.add("dve", lambda E, cwsl=cwsl: E.tensor_tensor(sc2[:, 8:16], cwsl, gbb[:],
                                                              op=ALU.add),
                  reads=["l2f_cwb1", "l2f_gbbb"], writes=["swg_s2"])
            p.add("dve", lambda E: E.reduce_max(sc1[:, 2:3], sc2[:, 8:16], axis=AX.X),
                  reads=["swg_s2"], writes=["swg_s1"])
            p.add("dve", lambda E: E.tensor_scalar(sc2[:, 16:24], sc2[:, 8:16], sc1[:, 2:3],
                                                   None, op0=ALU.is_equal),
                  reads=["swg_s1", "swg_s2"], writes=["swg_s2"])
            p.add("dve", lambda E: E.scalar_tensor_tensor(sc2[:, 24:32], sc2[:, 16:24], -1e30,
                                                          sc2[:, 8:16], op0=ALU.mult, op1=ALU.add),
                  reads=["swg_s2"], writes=["swg_s2"])
            p.add("dve", lambda E: E.reduce_max(sc1[:, 3:4], sc2[:, 24:32], axis=AX.X),
                  reads=["swg_s2"], writes=["swg_s1"])
            p.add("dve", lambda E: E.tensor_scalar(sc2[:, 32:40], sc2[:, 8:16], sc1[:, 3:4],
                                                   None, op0=ALU.is_ge),
                  reads=["swg_s1", "swg_s2"], writes=["swg_s2"])
            p.add("dve", lambda E, cwsl=cwsl: E.tensor_tensor(cwsl, cwsl, sc2[:, 32:40],
                                                              op=ALU.mult),
                  reads=["swg_s2", "l2f_cwb1"], writes=["l2f_cwb1"])
        for tt in range(16):
            pst, pstn = ctx.psum()
            p.add("pe", lambda E, tt=tt, pst=pst: E.transpose(
                pst[0:8, 0:128],
                cwt[:, tt * 8:(tt + 1) * 8], idfsb[:]),
                reads=["l2f_cwb1", "l2f_idfb"], writes=[pstn])
            p.add("dve", lambda E, tt=tt, pst=pst: E.tensor_copy(
                cwT[:, tt * 128:(tt + 1) * 128], pst[0:8, 0:128]),
                reads=[pstn], writes=["l2f_cwTb"])
        for th in range(2):
          for oc in range(NCH):
              for tcl in range(2):
                  p.add("dve", lambda E, oc=oc, tcl=tcl: E.memset(
                      racc[:, oc * 1024 + tcl * 512: oc * 1024 + (tcl + 1) * 512], 0.0),
                      writes=[f"l2f_ra{oc}_{tcl}"])
          for e in range(9):
              p.dma_group("pool", [(w2sb[:, j * 1024:(j + 1) * 1024],
                                    w2_d[e, j * 128:(j + 1) * 128, :], {}) for j in range(8)],
                          writes=["l2f_w2b"])
              for pb in range(8):
                  wb = w13[pb % 2]
                  p.dma_group("pool",
                              [(wb[:, j * 256:(j + 1) * 256],
                                w1_d[e, j * 128:(j + 1) * 128, pb * 256:(pb + 1) * 256], {})
                               for j in range(NCH)]
                              + [(wb[:, 2048 + j * 256: 2048 + (j + 1) * 256],
                                  w3_d[e, j * 128:(j + 1) * 128, pb * 256:(pb + 1) * 256], {})
                                 for j in range(NCH)],
                              writes=[f"l2f_w13b{pb % 2}"])
                  for tcl in range(2):
                      tc = th * 2 + tcl
                      for gu, dst, dstb in ((0, sc1, "swg_g"), (1, sc2, "swg_l")):
                          ps1a, ps1n = ctx.psum()
                          ps3a, ps3n = ctx.psum()

                          def hmm(E, gu=gu, tc=tc, wb=wb, psx=None, w3off=0):
                              last = None
                              for ci in range(NCH):
                                  last = E.matmul(psx[:, 0:512],
                                                  wb[:, w3off + ci * 256 + gu * 128:
                                                        w3off + ci * 256 + gu * 128 + 128],
                                                  xhi[:, ci * T + tc * 512: ci * T + (tc + 1) * 512],
                                                  start=(ci == 0), stop=(ci == NCH - 1))
                              return last
                          p.add("pe", lambda E, f=hmm, ps1a=ps1a: f(E, psx=ps1a),
                                reads=[f"l2f_w13b{pb % 2}"] + [xb(c, tc) for c in range(NCH)],
                                writes=[ps1n])
                          p.add("pe", lambda E, f=hmm, ps3a=ps3a: f(E, psx=ps3a, w3off=2048),
                                reads=[f"l2f_w13b{pb % 2}"] + [xb(c, tc) for c in range(NCH)],
                                writes=[ps3n])
                          bix = e * 16 + pb * 2 + gu
                          p.add("act", lambda E, bix=bix, ps1a=ps1a: E.activation(
                              hsb[:, 0:512], ps1a[:, 0:512], AF.Identity, bias=bcols[:, bix:bix + 1]),
                              reads=[ps1n, "l2f_bcb"], writes=["l2f_h1"])
                          p.add("act", lambda E, bix=bix, ps3a=ps3a: E.activation(
                              hsb[:, 512:1024], ps3a[:, 0:512], AF.Identity,
                              bias=bcols[:, 144 + bix: 144 + bix + 1]),
                              reads=[ps3n, "l2f_bcb"], writes=["l2f_h3"])
                          p.add("dve", lambda E, dst=dst: E.tensor_tensor(
                              dst[:, 0:512], hsb[:, 0:512], hsb[:, 512:1024], op=ALU.mult),
                              reads=["l2f_h1", "l2f_h3"], writes=[dstb])
                      p.add("dve", lambda E: E.tensor_scalar_min(sc1[:, 0:512], sc1[:, 0:512], 7.0),
                            reads=["swg_g"], writes=["swg_g"])
                      p.add("act", lambda E: E.activation(hsb[:, 0:512], sc1[:, 0:512],
                                                          AF.Sigmoid, scale=1.702),
                            reads=["swg_g"], writes=["l2f_h1"])
                      p.add("dve", lambda E: E.tensor_tensor(sc1[:, 0:512], sc1[:, 0:512],
                                                             hsb[:, 0:512], op=ALU.mult),
                            reads=["swg_g", "l2f_h1"], writes=["swg_g"])
                      p.add("dve", lambda E: E.tensor_scalar(sc2[:, 0:512], sc2[:, 0:512], -7.0, 7.0,
                                                             op0=ALU.max, op1=ALU.min),
                            reads=["swg_l"], writes=["swg_l"])
                      p.add("dve", lambda E: E.tensor_scalar_add(sc2[:, 0:512], sc2[:, 0:512], 1.0),
                            reads=["swg_l"], writes=["swg_l"])
                      p.add("dve", lambda E, pb=pb, tcl=tcl: E.tensor_tensor(
                          asb[:, tcl * 4096 + pb * 512: tcl * 4096 + (pb + 1) * 512],
                          sc1[:, 0:512], sc2[:, 0:512],
                          op=ALU.mult), reads=["swg_g", "swg_l"], writes=[f"l2f_a{tcl}_{pb}"])
              for tcl in range(2):
                  tc = th * 2 + tcl
                  if e < 8:
                      psc, pscn = ctx.psum()
                      p.add("pe", lambda E, e=e, tc=tc, psc=psc: E.matmul(
                          psc[:, 0:512], selsb[:, e * 128:(e + 1) * 128],
                          cwT[:, tc * 512:(tc + 1) * 512], start=True, stop=True),
                          reads=["l2f_selb", "l2f_cwTb"], writes=[pscn])
                      p.add("dve", lambda E, psc=psc: E.tensor_copy(cwb[:], psc[:, 0:512]),
                            reads=[pscn], writes=["l2f_cwbb"])
                  for dt_ in range(NCH):
                      psd, psdn = ctx.psum()

                      def dmm(E, dt_=dt_, tcl=tcl, psd=psd):
                          last = None
                          for j in range(8):
                              last = E.matmul(psd[:, 0:512],
                                              w2sb[:, j * 1024 + dt_ * 128: j * 1024 + dt_ * 128 + 128],
                                              asb[:, tcl * 4096 + j * 512: tcl * 4096 + (j + 1) * 512],
                                              start=(j == 0), stop=(j == 7))
                          return last
                      p.add("pe", dmm, reads=["l2f_w2b"] + [f"l2f_a{tcl}_{j}" for j in range(8)],
                            writes=[psdn])
                      bix = e * 8 + dt_
                      p.add("dve", lambda E, bix=bix, psd=psd: E.tensor_scalar(
                          psd[:, 0:512], psd[:, 0:512], bcols[:, 288 + bix: 288 + bix + 1], None,
                          op0=ALU.add), reads=[psdn, "l2f_bcb"], writes=[psdn])
                      rsl = racc[:, dt_ * 1024 + tcl * 512: dt_ * 1024 + (tcl + 1) * 512]
                      if e < 8:
                          p.add("dve", lambda E, psd=psd: E.tensor_tensor(
                              sc1[:, 0:512], psd[:, 0:512], cwb[:], op=ALU.mult),
                              reads=[psdn, "l2f_cwbb"], writes=["swg_g"])
                          p.add("dve", lambda E, rsl=rsl, dt_=dt_, tcl=tcl: E.tensor_tensor(
                              rsl, rsl, sc1[:, 0:512], op=ALU.add),
                              reads=["swg_g", f"l2f_ra{dt_}_{tcl}"], writes=[f"l2f_ra{dt_}_{tcl}"])
                      else:
                          p.add("dve", lambda E, rsl=rsl, dt_=dt_, tcl=tcl, psd=psd: E.tensor_tensor(
                              rsl, rsl, psd[:, 0:512], op=ALU.add),
                              reads=[psdn, f"l2f_ra{dt_}_{tcl}"], writes=[f"l2f_ra{dt_}_{tcl}"])
          for tcl in range(2):
              tc = th * 2 + tcl
              ws = wss[0]
              load_x_window(ctx, ws, x_in[item], tc * 512)
              for oc in range(NCH):
                  p.add("dve", lambda E, oc=oc, tcl=tcl, ws=ws: E.tensor_tensor(
                      ws.x[:, oc * 512:(oc + 1) * 512], ws.x[:, oc * 512:(oc + 1) * 512],
                      racc[:, oc * 1024 + tcl * 512: oc * 1024 + (tcl + 1) * 512], op=ALU.add),
                      reads=[f"l2f_ra{oc}_{tcl}", ws.b("x", oc)], writes=[ws.b("x", oc)])
              store_x_window(ctx, ws, x_out[item], tc * 512)



def host_l3f(inputs):
    P = _perm_gu(2048)
    gup = np.asarray(inputs["L3_ffn_gate_up_proj"], np.float32)[:, :, P]
    gub = np.asarray(inputs["L3_ffn_gate_up_proj_bias"], np.float32)[:, P]
    dn = np.asarray(inputs["L3_ffn_down_proj"], np.float32)
    dnb = np.asarray(inputs["L3_ffn_down_proj_bias"], np.float32)
    idx = np.asarray(inputs["L3_ffn_router_indices"], np.int32)
    rw = np.asarray(inputs["L3_ffn_routing_weights"], np.float32)
    sel = np.zeros((8, 8 * 128), np.float32)
    for e in range(8):
        sel[e, e * 128:(e + 1) * 128] = 1.0
    return {
        "l3f_gup": gup, "l3f_dn": dn,
        "l3f_gubc": np.ascontiguousarray(gub.reshape(8, 16, 128).transpose(2, 0, 1).reshape(128, 128)),
        "l3f_dnbc": np.ascontiguousarray(dnb.reshape(8, 8, 128).transpose(2, 0, 1).reshape(128, 64)),
        "l3f_idx": [np.ascontiguousarray(idx[c * 256:(c + 1) * 256]) for c in range(8)],
        "l3f_rw": [np.ascontiguousarray(rw[c * 256:(c + 1) * 256]) for c in range(8)],
        "l3f_sel": sel,
        "l3f_id128f": np.eye(128, dtype=np.float32),
    }


def emit_l3f(ctx, x_in, x_out, blk):
    p = ctx.p
    nc = ctx.nc
    gup_d = ctx.win("l3f_gup")
    dn_d = ctx.win("l3f_dn")
    gub_d = ctx.win("l3f_gubc")
    dnb_d = ctx.win("l3f_dnbc")
    idx_d = ctx.win("l3f_idx")
    rw_d = ctx.win("l3f_rw")
    sel_d = ctx.win("l3f_sel")
    idf_d = ctx.win("l3f_id128f")
    xb0_d = ctx.win("l3f_xb0")
    y_d = ctx.y_out_ap

    TW = 256
    gup_sb = ctx.sbuf("l3f_gups", [128, 8 * 2048], BF16)
    dn_sb = ctx.sbuf("l3f_dns", [128, 8 * 1024], BF16)
    xhs = ctx.sbuf("l3f_xh", [128, 8 * TW], BF16)
    racc = ctx.sbuf("l3f_racc", [128, 8 * TW], F32)
    asb = ctx.sbuf("l3f_asb", [128, 8 * TW], BF16)
    sc1 = ctx.sbuf("swg_sc1", [128, 512], F32)
    sc2 = ctx.sbuf("swg_sc2", [128, 512], F32)
    hsb = ctx.sbuf("l2f_hsb", [128, 2 * 512], F32)
    bcg = ctx.sbuf("l3f_bcg", [128, 128], F32)
    bcd = ctx.sbuf("l3f_bcd", [128, 64], F32)
    idxsb = ctx.sbuf("l3f_idxs", [128, 4], I32)
    idxf = ctx.sbuf("l3f_idxf", [128, 4], F32)
    rwsb = ctx.sbuf("l3f_rws", [128, 16], F32)
    cw3 = ctx.sbuf("l3f_cw3", [128, 16], F32)
    cwT = ctx.sbuf("l3f_cwT", [8, 256], F32)
    cwb = ctx.sbuf("l3f_cwb", [128, 256], F32)
    selsb = ctx.sbuf("l2f_sels", [8, 8 * 128], F32)
    idfsb = ctx.sbuf("l2f_idfs", [128, 128], F32)
    ws = WS(ctx, "l3f_ws", TW, need_xh=False)

    for item in range(2):
        p.dma_group("sp", [(x_out[item][c * 128:(c + 1) * 128, :],
                            x_in[item][c * 128:(c + 1) * 128, :], {}) for c in range(NCH)])
    p.dma_group("sp", [(bcg[:], gub_d, {}), (bcd[:], dnb_d, {}), (selsb[:], sel_d, {}),
                       (idfsb[:], idf_d, {})],
                writes=["l3f_bcb", "l2f_selb", "l2f_idfb"])
    p.dma_group("sp", [(idxsb[0:128, 0:2], idx_d[0:128, :], {}),
                       (idxsb[0:128, 2:4], idx_d[128:256, :], {}),
                       (rwsb[0:128, 0:8], rw_d[0:128, :], {}),
                       (rwsb[0:128, 8:16], rw_d[128:256, :], {})], writes=["l3f_idxb"])
    p.add("dve", lambda E: E.tensor_copy(idxf[:], idxsb[:]), reads=["l3f_idxb"],
          writes=["l3f_idxfb"])
    for half in range(2):
        for e in range(8):
            p.add("dve", lambda E, half=half, e=e: E.tensor_scalar(
                sc1[:, 0:2], idxf[:, half * 2: half * 2 + 2], float(e), None, op0=ALU.is_equal),
                reads=["l3f_idxfb"], writes=["swg_s1"])
            p.add("dve", lambda E: E.reduce_sum(sc1[:, 2:3], sc1[:, 0:2], axis=AX.X),
                  reads=["swg_s1"], writes=["swg_s1"])
            p.add("dve", lambda E, half=half, e=e: E.tensor_scalar_mul(
                cw3[:, half * 8 + e: half * 8 + e + 1], rwsb[:, half * 8 + e: half * 8 + e + 1],
                sc1[:, 2:3]), reads=["swg_s1", "l3f_idxb"], writes=["l3f_cw3b"])
    for half in range(2):
        pst, pstn = ctx.psum()
        p.add("pe", lambda E, half=half, pst=pst: E.transpose(
            pst[0:8, 0:128], cw3[:, half * 8:(half + 1) * 8], idfsb[:]),
            reads=["l3f_cw3b", "l2f_idfb"], writes=[pstn])
        p.add("dve", lambda E, half=half, pst=pst: E.tensor_copy(
            cwT[:, half * 128:(half + 1) * 128], pst[0:8, 0:128]),
            reads=[pstn], writes=["l3f_cwTb"])
    items = [(ws.x[:, c * TW:(c + 1) * TW], xb0_d[c * 128:(c + 1) * 128, :], {})
             for c in range(NCH)]
    p.dma_group("sp", items, writes=[ws.b("x", c) for c in range(NCH)])
    emit_ln(ctx, ws, out_aps=[xhs[:, c * TW:(c + 1) * TW] for c in range(NCH)],
            out_bufs=[f"l3fxh{c}" for c in range(NCH)])
    for oc in range(NCH):
        p.add("dve", lambda E, oc=oc: E.memset(racc[:, oc * TW:(oc + 1) * TW], 0.0),
              writes=[f"l3f_ra{oc}"])
    for e in range(8):
        p.dma_group("pool", [(gup_sb[:, j * 2048:(j + 1) * 2048], gup_d[e, j * 128:(j + 1) * 128, :],
                              {}) for j in range(NCH)], writes=["l3f_gupb"])
        p.dma_group("pool", [(dn_sb[:, j * 1024:(j + 1) * 1024], dn_d[e, j * 128:(j + 1) * 128, :],
                              {}) for j in range(NCH)], writes=["l3f_dnb"])
        psc, pscn = ctx.psum()
        p.add("pe", lambda E, e=e, psc=psc: E.matmul(
            psc[:, 0:256], selsb[:, e * 128:(e + 1) * 128], cwT[:, 0:256],
            start=True, stop=True), reads=["l2f_selb", "l3f_cwTb"], writes=[pscn])
        p.add("dve", lambda E, psc=psc: E.tensor_copy(cwb[:, 0:256], psc[:, 0:256]),
              reads=[pscn], writes=["l3f_cwbb"])
        for pb in range(8):
            for gu in range(2):
                ft = pb * 2 + gu
                psa, psan = ctx.psum()

                def gmm(E, ft=ft, psa=psa):
                    last = None
                    for ci in range(NCH):
                        last = E.matmul(psa[:, 0:TW],
                                        gup_sb[:, ci * 2048 + ft * 128: ci * 2048 + ft * 128 + 128],
                                        xhs[:, ci * TW:(ci + 1) * TW],
                                        start=(ci == 0), stop=(ci == NCH - 1))
                    return last
                p.add("pe", gmm, reads=["l3f_gupb"] + [f"l3fxh{c}" for c in range(NCH)],
                      writes=[psan])
                bix = e * 16 + ft
                dstb = "swg_g" if gu == 0 else "swg_l"
                dst = sc1 if gu == 0 else sc2
                p.add("act", lambda E, bix=bix, psa=psa, dst=dst: E.activation(
                    dst[:, 0:TW], psa[:, 0:TW], AF.Identity, bias=bcg[:, bix:bix + 1]),
                    reads=[psan, "l3f_bcb"], writes=[dstb])
            p.add("dve", lambda E: E.tensor_scalar_min(sc1[:, 0:TW], sc1[:, 0:TW], 7.0),
                  reads=["swg_g"], writes=["swg_g"])
            p.add("act", lambda E: E.activation(hsb[:, 0:TW], sc1[:, 0:TW], AF.Sigmoid,
                                                scale=1.702), reads=["swg_g"], writes=["l2f_h1"])
            p.add("dve", lambda E: E.tensor_tensor(sc1[:, 0:TW], sc1[:, 0:TW], hsb[:, 0:TW],
                                                   op=ALU.mult), reads=["swg_g", "l2f_h1"],
                  writes=["swg_g"])
            p.add("dve", lambda E: E.tensor_scalar(sc2[:, 0:TW], sc2[:, 0:TW], -7.0, 7.0,
                                                   op0=ALU.max, op1=ALU.min),
                  reads=["swg_l"], writes=["swg_l"])
            p.add("dve", lambda E: E.tensor_scalar_add(sc2[:, 0:TW], sc2[:, 0:TW], 1.0),
                  reads=["swg_l"], writes=["swg_l"])
            p.add("dve", lambda E, pb=pb: E.tensor_tensor(
                asb[:, pb * TW:(pb + 1) * TW], sc1[:, 0:TW], sc2[:, 0:TW], op=ALU.mult),
                reads=["swg_g", "swg_l"], writes=[f"l3f_a{pb}"])
        for dt_ in range(NCH):
            psd, psdn = ctx.psum()

            def dmm(E, dt_=dt_, psd=psd):
                last = None
                for j in range(8):
                    last = E.matmul(psd[:, 0:TW],
                                    dn_sb[:, j * 1024 + dt_ * 128: j * 1024 + dt_ * 128 + 128],
                                    asb[:, j * TW:(j + 1) * TW],
                                    start=(j == 0), stop=(j == 7))
                return last
            p.add("pe", dmm, reads=["l3f_dnb"] + [f"l3f_a{j}" for j in range(8)], writes=[psdn])
            bix = e * 8 + dt_
            p.add("dve", lambda E, bix=bix, psd=psd: E.tensor_scalar(
                psd[:, 0:TW], psd[:, 0:TW], bcd[:, bix:bix + 1], None, op0=ALU.add),
                reads=[psdn, "l3f_bcb"], writes=[psdn])
            p.add("dve", lambda E, psd=psd: E.tensor_tensor(sc1[:, 0:TW], psd[:, 0:TW],
                                                            cwb[:, 0:256], op=ALU.mult),
                  reads=[psdn, "l3f_cwbb"], writes=["swg_g"])
            p.add("dve", lambda E, dt_=dt_: E.tensor_tensor(
                racc[:, dt_ * TW:(dt_ + 1) * TW], racc[:, dt_ * TW:(dt_ + 1) * TW],
                sc1[:, 0:TW], op=ALU.add), reads=["swg_g", f"l3f_ra{dt_}"],
                writes=[f"l3f_ra{dt_}"])
    p.dma_group("sp", [(y_d[c * 128:(c + 1) * 128, :], racc[:, c * TW:(c + 1) * TW], {})
                       for c in range(NCH)], reads=[f"l3f_ra{c}" for c in range(NCH)])


def emit_l3fadd(ctx, x_in, x_out, blk):
    p = ctx.p
    yad = ctx.win("l3f_yadd")
    ws = WS(ctx, "att_ws0", 512, need_xh=False)
    ya = ctx.sbuf("l3fa_y", [128, NCH * 512], F32)
    for item in range(2):
        for w in range(4):
            load_x_window(ctx, ws, x_in[item], w * 512)
            p.dma_group("sp", [(ya[:, c * 512:(c + 1) * 512],
                                yad[item][c * 128:(c + 1) * 128, w * 512:(w + 1) * 512], {})
                               for c in range(NCH)],
                        writes=[f"l3fa_y{c}" for c in range(NCH)])
            for c in range(NCH):
                p.add("dve", lambda E, c=c: E.tensor_tensor(
                    ws.x[:, c * 512:(c + 1) * 512], ws.x[:, c * 512:(c + 1) * 512],
                    ya[:, c * 512:(c + 1) * 512], op=ALU.add),
                    reads=[f"l3fa_y{c}", ws.b("x", c)], writes=[ws.b("x", c)])
            store_x_window(ctx, ws, x_out[item], w * 512)



def host_l1a(inputs):
    wq = np.asarray(inputs["L1_attn_w_qkv"], np.float32)
    wo = np.asarray(inputs["L1_attn_w_out"], np.float32)
    lg = np.asarray(inputs["L1_attn_ln_g"], np.float32)
    lb = np.asarray(inputs["L1_attn_ln_b"], np.float32)
    mb = np.asarray(inputs["L1_attn_mem_beta"], np.float32).reshape(1, 16)
    d = 64
    freqs = 10000.0 ** (-np.arange(0, d, 2, dtype=np.float64) / d)
    ang = np.arange(T, dtype=np.float64)[:, None] * freqs[None, :]
    cos = np.repeat(np.cos(ang), 2, axis=1).astype(np.float32)
    sin = np.repeat(np.sin(ang), 2, axis=1).astype(np.float32)
    cosd = np.zeros((128, T), np.float32)
    sind = np.zeros((128, T), np.float32)
    cosd[0:64] = cos.T
    cosd[64:128] = cos.T
    sind[0:64] = sin.T
    sind[64:128] = sin.T
    R = np.zeros((64, 64), np.float32)
    for i in range(32):
        R[2 * i, 2 * i + 1] = -1.0
        R[2 * i + 1, 2 * i] = 1.0
    Rt = np.zeros((128, 128), np.float32)
    Rt[0:64, 0:64] = R.T
    Rt[64:128, 64:128] = R.T
    mask128 = np.full((128, 128), -30000.0, np.float32)
    for s_ in range(4):
        mask128[s_ * 32:(s_ + 1) * 32, s_ * 32:(s_ + 1) * 32] = \
            np.triu(np.full((32, 32), -30000.0, np.float32), 1)
    return {"l1a_wq": wq, "l1a_wo": wo,
            "l1a_lg": np.ascontiguousarray(lg.reshape(8, 128).T),
            "l1a_lb": np.ascontiguousarray(lb.reshape(8, 128).T),
            "l1a_mb": mb, "l1a_cos": cosd, "l1a_sin": sind,
            "l1a_rt": Rt.astype(BF), "l1a_mask": mask128}


def emit_l1a(ctx, x_in, x_out, blk):
    p = ctx.p
    wq_d = ctx.win("l1a_wq")
    wo_d = ctx.win("l1a_wo")
    lg_d = ctx.win("l1a_lg")
    lb_d = ctx.win("l1a_lb")
    mb_d = ctx.win("l1a_mb")
    cos_d = ctx.win("l1a_cos")
    sin_d = ctx.win("l1a_sin")
    rt_d = ctx.win("l1a_rt")
    mask_d = ctx.win("l1a_mask")

    wq_sb = ctx.sbuf("l1a_wqs", [128, NCH * 3072], BF16)
    wo_sb = ctx.sbuf("l1a_wos", [128, NCH * 1024], BF16)
    lgc = ctx.sbuf("l1a_lgcs", [128, 16], F32)
    xh = ctx.sbuf("l1a_xh", [128, NCH * 512], BF16)
    qA = ctx.sbuf("l1a_qA", [128, 8 * 512], BF16)
    kB = ctx.sbuf("l1a_kB", [128, 8 * 512], BF16)
    qr = ctx.sbuf("l1a_qr", [128, 8 * 512], BF16)
    kr = ctx.sbuf("l1a_kr", [128, 8 * 512], BF16)
    vaug = ctx.sbuf("l1a_va", [128, 4 * 1040], BF16)
    sktok = [ctx.sbuf(f"l1a_skt{i}", [32, 8 * 128], BF16) for i in range(2)]
    sdpa = ctx.sbuf("l1a_sd", [128, 4 * 1024], BF16)
    otok = ctx.sbuf("l1a_ot", [128, 4 * 1024], BF16)
    ofm = qr
    mem_all = ctx.sbuf("l1a_mem", [128, 8 * 130], F32)
    memb = ctx.sbuf("l1a_memb", [128, 8 * 130], BF16)
    ones32 = ctx.sbuf("l1a_one32", [32, 1], BF16)
    vsub = ctx.sbuf("l1a_vs", [32, 8 * 128], BF16)
    cosb = ctx.sbuf("l1a_cosbs", [128, T], BF16)
    sinb = ctx.sbuf("l1a_sinbs", [128, T], BF16)
    rtb = ctx.sbuf("l1a_rtbs", [128, 128], BF16)
    maskb = ctx.sbuf("l1a_maskbs", [128, 128], F32)
    bb = ctx.sbuf("l1a_bb", [128, 2 * 1024], BF16)
    Pb = ctx.sbuf("l1a_P", [128, 128], BF16)
    PTb = ctx.sbuf("l1a_PT", [128, 128], BF16)
    osc = ctx.sbuf("l1a_osc", [128, 2 * 1024], F32)
    sc1 = ctx.sbuf("swg_sc1", [128, 512], F32)
    sc2 = ctx.sbuf("swg_sc2", [128, 512], F32)
    mcol = ctx.sbuf("l1a_mc", [128, 2], F32)
    wss = [WS(ctx, "att_ws0", 512, need_xh=False)]
    brow = wss[0].x

    p.dma_group("pool", [(wq_sb[:, ci * 3072:(ci + 1) * 3072],
                          wq_d[ci * 128:(ci + 1) * 128, :], {}) for ci in range(NCH)],
                writes=["l1a_wqb"])
    p.dma_group("pool", [(wo_sb[:, ci * 1024:(ci + 1) * 1024],
                          wo_d[ci * 128:(ci + 1) * 128, :], {}) for ci in range(NCH)]
                + [(cosb[:], cos_d, {}), (sinb[:], sin_d, {})],
                writes=["l1a_wob", "l1a_cosbb"])
    p.dma_group("sp", [(lgc[:, 0:8], lg_d, {}), (lgc[:, 8:16], lb_d, {}),
                       (rtb[:], rt_d, {}), (maskb[:], mask_d, {}),
                       (brow[0:1, 0:16], mb_d, {})],
                writes=["l1a_lgb", "l1a_rtbb", "l1a_maskbb", "l1a_brb"])
    p.add("act", lambda E: E.activation(brow[0:1, 16:32], brow[0:1, 0:16], AF.Sigmoid),
          reads=["l1a_brb"], writes=["l1a_brb"])
    p.add("dve", lambda E: E.tensor_copy(
        ap_dims(brow[0:1, 1024:1025], [list(brow[0:1, 0:1].ap[0]), [64, 16], [1, 64]]),
        ap_dims(brow[0:1, 16:17], [list(brow[0:1, 0:1].ap[0]), [1, 16], [0, 64]])),
        reads=["l1a_brb"], writes=["l1a_brb2"])
    p.add("dve", lambda E: E.tensor_scalar(brow[0:1, 0:1024], brow[0:1, 1024:2048],
                                           -1.0, 1.0, op0=ALU.mult, op1=ALU.add),
          reads=["l1a_brb2"], writes=["l1a_brb3"])
    for j, (srcoff, dstoff, bname) in enumerate(((1024, 0, "l1a_brb2"), (1536, 512, "l1a_brb2"),
                                                 (0, 1024, "l1a_brb3"), (512, 1536, "l1a_brb3"))):
        psx, psxn = ctx.psum()
        p.add("pe", lambda E, srcoff=srcoff, psx=psx: E.matmul(
            psx[:, 0:512], ctx.onesrow[0:1, :], brow[0:1, srcoff:srcoff + 512],
            start=True, stop=True), reads=[bname, "c_onesrow"], writes=[psxn])
        p.add("dve", lambda E, dstoff=dstoff, psx=psx: E.tensor_copy(
            bb[:, dstoff:dstoff + 512], psx[:, 0:512]), reads=[psxn],
            writes=["l1a_bbb"] + [wss[0].b("x", c) for c in range(NCH)])

    p.add("dve", lambda E: E.memset(ones32[:], 1.0), writes=["l1a_ones32"])
    for item in range(2):
        it = f"{blk}i{item}"
        p.add("dve", lambda E: E.memset(mem_all[:], 0.0), writes=["l1a_memall"])
        p.add("dve", lambda E: E.memset(
            ap_dims(mem_all[0:64, 128:129], [list(mem_all[0:64, 0:1].ap[0]), [130, 8], [1, 1]]),
            1e-6), reads=["l1a_memall"], writes=["l1a_memall"])
        p.add("dve", lambda E: E.memset(
            ap_dims(mem_all[64:128, 129:130], [list(mem_all[64:128, 0:1].ap[0]), [130, 8], [1, 1]]),
            1e-6), reads=["l1a_memall"], writes=["l1a_memall"])
        p.add("dve", lambda E: E.tensor_copy(memb[:], mem_all[:]),
              reads=["l1a_memall"], writes=["l1a_membb"])
        for cc in range(4):
            t0 = cc * 512
            ws = wss[0]
            load_x_window(ctx, ws, x_in[item], t0)
            emit_ln(ctx, ws, gain=lgc[:, 0:8], bias=lgc[:, 8:16], gbuf="l1a_lgb",
                    out_aps=[xh[:, c * 512:(c + 1) * 512] for c in range(NCH)],
                    out_bufs=[f"{it}xh{c}" for c in range(NCH)])
            for pp in range(8):
                for dst, dstb, off in ((qA, "l1a_qAb", 0), (kB, "l1a_kBb", 1024)):
                    psq, psqn = ctx.psum()

                    def qmm(E, pp=pp, off=off, psq=psq):
                        last = None
                        for ci in range(NCH):
                            last = E.matmul(psq[:, 0:512],
                                            wq_sb[:, ci * 3072 + off + pp * 128:
                                                  ci * 3072 + off + pp * 128 + 128],
                                            xh[:, ci * 512:(ci + 1) * 512],
                                            start=(ci == 0), stop=(ci == NCH - 1))
                        return last
                    p.add("pe", qmm, reads=["l1a_wqb"] + [f"{it}xh{c}" for c in range(NCH)],
                          writes=[psqn])
                    p.add("act", lambda E, pp=pp, dst=dst, psq=psq: E.activation(
                        dst[:, pp * 512:(pp + 1) * 512], psq[:, 0:512], AF.Copy),
                        reads=[psqn], writes=[dstb])
                for src, srcb, dst, dstb in ((qA, "l1a_qAb", qr, "l1a_qrb"),
                                             (kB, "l1a_kBb", kr, "l1a_krb")):
                    psr, psrn = ctx.psum()
                    p.add("pe", lambda E, pp=pp, src=src, psr=psr: E.matmul(
                        psr[:, 0:512], rtb[:], src[:, pp * 512:(pp + 1) * 512],
                        start=True, stop=True), reads=[srcb, "l1a_rtbb"], writes=[psrn])
                    p.add("dve", lambda E, pp=pp, src=src, t0=t0: E.tensor_tensor(
                        sc1[:, 0:512], src[:, pp * 512:(pp + 1) * 512],
                        cosb[:, t0:t0 + 512], op=ALU.mult),
                        reads=[srcb, "l1a_cosbb"], writes=["swg_s1"])
                    p.add("dve", lambda E, t0=t0, psr=psr: E.tensor_tensor(
                        sc2[:, 0:512], psr[:, 0:512], sinb[:, t0:t0 + 512], op=ALU.mult),
                        reads=[psrn, "l1a_cosbb"], writes=["swg_s2"])
                    wr = [dstb] + (["l1a_ofb"] if dst is qr else [])
                    p.add("dve", lambda E, pp=pp, dst=dst: E.tensor_tensor(
                        dst[:, pp * 512:(pp + 1) * 512], sc1[:, 0:512], sc2[:, 0:512],
                        op=ALU.add), reads=["swg_s1", "swg_s2"], writes=wr)
                for srcd, srcb in ((qA, "l1a_qAb"), (kB, "l1a_kBb")):
                    p.add("dve", lambda E, pp=pp, srcd=srcd: E.tensor_scalar_min(
                        sc1[:, 0:512], srcd[:, pp * 512:(pp + 1) * 512], 0.0),
                        reads=[srcb], writes=["swg_s1"])
                    p.add("act", lambda E: E.activation(sc1[:, 0:512], sc1[:, 0:512], AF.Exp),
                          reads=["swg_s1"], writes=["swg_s1"])
                    p.add("dve", lambda E, pp=pp, srcd=srcd: E.tensor_relu(
                        sc2[:, 0:512], srcd[:, pp * 512:(pp + 1) * 512]),
                        reads=[srcb], writes=["swg_s2"])
                    p.add("dve", lambda E, pp=pp, srcd=srcd: E.tensor_tensor(
                        srcd[:, pp * 512:(pp + 1) * 512], sc1[:, 0:512], sc2[:, 0:512],
                        op=ALU.add), reads=["swg_s1", "swg_s2"], writes=[srcb])
            p.add("dve", lambda E: E.memset(
                ap_dims(vaug[:, 64:65], [list(vaug[:, 0:1].ap[0]), [1040, 4], [65, 16], [1, 1]]),
                1.0), writes=["l1a_vones"])
            for tt in range(4):
                for nb in range(2):
                    psv, psvn = ctx.psum()

                    def vmm(E, tt=tt, nb=nb, psv=psv):
                        last = None
                        for ci in range(NCH):
                            last = E.matmul(psv[:, 0:512],
                                            xh[:, ci * 512 + tt * 128: ci * 512 + (tt + 1) * 128],
                                            wq_sb[:, ci * 3072 + 2048 + nb * 512:
                                                  ci * 3072 + 2048 + (nb + 1) * 512],
                                            start=(ci == 0), stop=(ci == NCH - 1))
                        return last
                    p.add("pe", vmm, reads=["l1a_wqb"] + [f"{it}xh{c}" for c in range(NCH)],
                          writes=[psvn])
                    for hh in range(8):
                        h = nb * 8 + hh
                        p.add("dve", lambda E, tt=tt, h=h, hh=hh, psv=psv: E.tensor_copy(
                            vaug[:, tt * 1040 + h * 65: tt * 1040 + h * 65 + 64],
                            psv[:, hh * 64:(hh + 1) * 64]),
                            reads=[psvn], writes=[f"l1a_vt{tt}h{h}"])
            for tt in range(4):
                for h in range(16):
                    pp, sub = divmod(h, 2)
                    pss = ctx.ps[0]
                    p.add("pe", lambda E, tt=tt, pp=pp, sub=sub, pss=pss: E.matmul(
                        pss[:, 0:128],
                        qr[sub * 64: sub * 64 + 64, pp * 512 + tt * 128: pp * 512 + (tt + 1) * 128],
                        kr[sub * 64: sub * 64 + 64, pp * 512 + tt * 128: pp * 512 + (tt + 1) * 128],
                        start=True, stop=True),
                        reads=["l1a_qrb", "l1a_krb"], writes=["ps0"])
                    p.add("dve", lambda E, pss=pss: E.tensor_tensor(
                        pss[:, 0:128], pss[:, 0:128], maskb[:], op=ALU.add),
                        reads=["l1a_maskbb", "ps0"], writes=["ps0"])
                    p.add("dve", lambda E, pss=pss: E.reduce_max(mcol[:, 0:1], pss[:, 0:128],
                                                                 axis=AX.X),
                          reads=["ps0"], writes=["l1a_mcb"])
                    p.add("dve", lambda E: E.tensor_scalar_mul(mcol[:, 1:2], mcol[:, 0:1],
                                                               -0.125),
                          reads=["l1a_mcb"], writes=["l1a_mcb"])
                    p.add("act", lambda E, pss=pss: E.activation(
                        Pb[:], pss[:, 0:128], AF.Exp, bias=mcol[:, 1:2], scale=0.125),
                        reads=["ps0", "l1a_mcb"], writes=["l1a_Pbb"])
                    p.add("pe", lambda E: E.transpose(ctx.psb[:, 0:128], Pb[:], ctx.ident[:]),
                          reads=["l1a_Pbb", "c_ident"], writes=["psb"])
                    p.add("dve", lambda E: E.tensor_copy(PTb[:], ctx.psb[:, 0:128]),
                          reads=["psb"], writes=["l1a_PTbb"])
                    pso = ctx.ps[1]
                    p.add("pe", lambda E, tt=tt, h=h, pso=pso: E.matmul(
                        pso[:, 0:65], PTb[:],
                        vaug[:, tt * 1040 + h * 65: tt * 1040 + h * 65 + 65],
                        start=True, stop=True),
                        reads=["l1a_PTbb", f"l1a_vt{tt}h{h}", "l1a_vones"], writes=["ps1"])
                    p.add("dve", lambda E, pso=pso: E.reciprocal(sc1[:, 0:1], pso[:, 64:65]),
                          reads=["ps1"], writes=["swg_s1"])
                    p.add("dve", lambda E, tt=tt, h=h, pso=pso: E.tensor_scalar_mul(
                        sdpa[:, tt * 1024 + h * 64: tt * 1024 + (h + 1) * 64],
                        pso[:, 0:64], sc1[:, 0:1]),
                        reads=["ps1", "swg_s1"], writes=[f"l1a_sdt{tt}"])
            for w in range(16):
                tt, sw = divmod(w, 4)
                rb = 32 * sw
                tcol = w * 32
                skt = sktok[w % 2]
                sktb = f"l1a_sktb{w % 2}"
                for pg in range(2):
                    for pp in range(4):
                        pp2 = pg * 4 + pp
                        p.add("pe", lambda E, pp2=pp2, pp=pp, tcol=tcol: E.transpose(
                            ctx.psb[0:32, pp * 128:(pp + 1) * 128],
                            kB[:, pp2 * 512 + tcol: pp2 * 512 + tcol + 32], ctx.ident[:]),
                            reads=["l1a_kBb", "c_ident"], writes=["psb"],
                            inc=(0 if pp < 3 else 1))
                    p.add("dve", lambda E, pg=pg, skt=skt: E.tensor_copy(
                        skt[0:32, pg * 512:(pg + 1) * 512], ctx.psb[0:32, 0:512]),
                        reads=["psb"], writes=[sktb])

                def sidemm(E, which, tcol=tcol):
                    src = qA if which == 0 else kB
                    banks = (ctx.ps[0], ctx.ps[1]) if which == 0 else (ctx.ps[2], ctx.ps[3])
                    last = None
                    for pp2 in range(8):
                        bank = banks[pp2 // 4]
                        off = (pp2 % 4) * 128
                        last = E.matmul(bank[0:32, off:off + 128],
                                        src[:, pp2 * 512 + tcol: pp2 * 512 + tcol + 32],
                                        memb[:, pp2 * 130: pp2 * 130 + 128],
                                        start=True, stop=True)
                    for pp2 in range(8):
                        last = E.matmul(ctx.ps[4][0:32, which * 16 + 2 * pp2:
                                                  which * 16 + 2 * pp2 + 2],
                                        src[:, pp2 * 512 + tcol: pp2 * 512 + tcol + 32],
                                        memb[:, pp2 * 130 + 128: pp2 * 130 + 130],
                                        start=True, stop=True)
                    return last
                p.add("pe", lambda E: sidemm(E, 0), reads=["l1a_qAb", "l1a_membb"],
                      writes=["ps0", "ps1", "ps4"])
                p.add("pe", lambda E: sidemm(E, 1), reads=["l1a_kBb", "l1a_membb"],
                      writes=["ps2", "ps3", "ps4"])
                p.add("dve", lambda E, rb=rb: E.tensor_scalar_add(sc1[rb:rb + 32, 0:16],
                                                                  ctx.ps[4][0:32, 0:16], 1e-6),
                      reads=["ps4"], writes=["swg_s1"])
                p.add("dve", lambda E, rb=rb: E.reciprocal(sc1[rb:rb + 32, 0:16],
                                                           sc1[rb:rb + 32, 0:16]),
                      reads=["swg_s1"], writes=["swg_s1"])
                p.add("dve", lambda E, rb=rb: E.reciprocal(sc1[rb:rb + 32, 16:32],
                                                           ctx.ps[4][0:32, 16:32]),
                      reads=["ps4"], writes=["swg_s1"])
                for bk in range(2):
                    rkb = ap_dims(sc1[rb:rb + 32, 16 + bk * 8: 16 + bk * 8 + 1],
                                  [list(sc1[rb:rb + 32, 0:1].ap[0]), [1, 8], [0, 64]])
                    p.add("dve", lambda E, bk=bk, rkb=rkb, rb=rb: E.tensor_tensor(
                        osc[rb:rb + 32, 1024 + bk * 512: 1024 + (bk + 1) * 512],
                        ctx.ps[2 + bk][0:32, 0:512], rkb, op=ALU.mult),
                        reads=[f"ps{2 + bk}", "swg_s1"], writes=["l1a_oscb"])
                vsl = ap_dims(vaug[rb:rb + 32, tt * 1040: tt * 1040 + 1],
                              [list(vaug[rb:rb + 32, 0:1].ap[0]), [65, 16], [1, 64]])
                p.add("dve", lambda E, vsl=vsl, rb=rb: E.tensor_tensor(
                    vsub[0:32, 0:1024], vsl, osc[rb:rb + 32, 1024:2048], op=ALU.subtract),
                    reads=["l1a_oscb"] + [f"l1a_vt{tt}h{hx}" for hx in range(16)],
                    writes=["l1a_vsb"])
                for bk in range(2):
                    rqb = ap_dims(sc1[rb:rb + 32, bk * 8: bk * 8 + 1],
                                  [list(sc1[rb:rb + 32, 0:1].ap[0]), [1, 8], [0, 64]])
                    p.add("dve", lambda E, bk=bk, rqb=rqb, rb=rb: E.tensor_tensor(
                        osc[rb:rb + 32, bk * 512:(bk + 1) * 512],
                        ctx.ps[0 + bk][0:32, 0:512], rqb, op=ALU.mult),
                        reads=[f"ps{0 + bk}", "swg_s1"], writes=["l1a_oscb"])
                p.add("dve", lambda E, rb=rb: E.tensor_tensor(
                    osc[rb:rb + 32, 0:1024], osc[rb:rb + 32, 0:1024],
                    bb[rb:rb + 32, 1024:2048], op=ALU.mult),
                    reads=["l1a_oscb", "l1a_bbb"], writes=["l1a_oscb"])
                p.add("dve", lambda E, tt=tt, rb=rb: E.tensor_tensor(
                    osc[rb:rb + 32, 1024:2048], sdpa[rb:rb + 32, tt * 1024:(tt + 1) * 1024],
                    bb[rb:rb + 32, 0:1024], op=ALU.mult),
                    reads=[f"l1a_sdt{tt}", "l1a_bbb"], writes=["l1a_oscb"])
                p.add("dve", lambda E, tt=tt, rb=rb: E.tensor_tensor(
                    otok[rb:rb + 32, tt * 1024:(tt + 1) * 1024], osc[rb:rb + 32, 0:1024],
                    osc[rb:rb + 32, 1024:2048], op=ALU.add),
                    reads=["l1a_oscb"], writes=[f"l1a_otb{tt}"])

                def umm(E, skt=skt):
                    last = None
                    for pp2 in range(8):
                        bank = ctx.ps[5 + pp2 // 4]
                        off = (pp2 % 4) * 128
                        last = E.matmul(bank[:, off:off + 128],
                                        skt[0:32, pp2 * 128:(pp2 + 1) * 128],
                                        vsub[0:32, pp2 * 128:(pp2 + 1) * 128],
                                        start=True, stop=True)
                    for pp2 in range(8):
                        last = E.matmul(ctx.ps[4][:, 32 + pp2: 33 + pp2],
                                        skt[0:32, pp2 * 128:(pp2 + 1) * 128],
                                        ones32[0:32, 0:1],
                                        start=True, stop=True)
                    return last
                p.add("pe", umm, reads=["l1a_vsb", sktb, "l1a_ones32"],
                      writes=["ps5", "ps6", "ps4"])
                for half, rlo, rhi in ((0, 0, 64), (1, 64, 128)):
                    for bk in range(2):
                        dstv = ap_dims(mem_all[rlo:rhi, bk * 4 * 130 + half * 64:
                                               bk * 4 * 130 + half * 64 + 1],
                                       [list(mem_all[rlo:rhi, 0:1].ap[0]), [130, 4], [1, 64]])
                        p.add("dve", lambda E, dstv=dstv, bk=bk, rlo=rlo, rhi=rhi,
                              half=half: E.tensor_tensor(
                            dstv, dstv, ap_dims(
                                ctx.ps[5 + bk][rlo:rhi, half * 64: half * 64 + 1],
                                [list(ctx.ps[5 + bk][rlo:rhi, 0:1].ap[0]), [128, 4], [1, 64]]),
                            op=ALU.add),
                            reads=[f"ps{5 + bk}", "l1a_memall"], writes=["l1a_memall"])
                    zdst = ap_dims(mem_all[rlo:rhi, 128 + half: 128 + half + 1],
                                   [list(mem_all[rlo:rhi, 0:1].ap[0]), [130, 8], [1, 1]])
                    p.add("dve", lambda E, zdst=zdst, rlo=rlo, rhi=rhi: E.tensor_tensor(
                        zdst, zdst, ctx.ps[4][rlo:rhi, 32:40], op=ALU.add),
                        reads=["ps4", "l1a_memall"], writes=["l1a_memall"])
                p.add("dve", lambda E: E.tensor_copy(memb[:], mem_all[:]),
                      reads=["l1a_memall"], writes=["l1a_membb"])
            for tt in range(4):
                for ci in range(8):
                    p.add("pe", lambda E, tt=tt, ci=ci: E.transpose(
                        ctx.psb[:, (ci % 4) * 128:(ci % 4) * 128 + 128],
                        otok[:, tt * 1024 + ci * 128: tt * 1024 + (ci + 1) * 128],
                        ctx.ident[:]),
                        reads=[f"l1a_otb{tt}", "c_ident"], writes=["psb"],
                        inc=(0 if ci % 4 < 3 else 1))
                    if ci % 4 == 3:
                        g0 = ci - 3
                        p.add("dve", lambda E, tt=tt, g0=g0: E.tensor_copy(
                            ap_dims(ofm[:, g0 * 512 + tt * 128: g0 * 512 + tt * 128 + 1],
                                    [list(ofm[:, 0:1].ap[0]), [512, 4], [1, 128]]),
                            ctx.psb[:, 0:512]), reads=["psb"],
                            writes=["l1a_ofb", "l1a_qrb"])
            load_x_window(ctx, ws, x_in[item], t0)
            for oc in range(NCH):
                pso2, pso2n = ctx.psum()

                def pmm(E, oc=oc, pso2=pso2):
                    last = None
                    for ci in range(NCH):
                        last = E.matmul(pso2[:, 0:512],
                                        wo_sb[:, ci * 1024 + oc * 128: ci * 1024 + oc * 128 + 128],
                                        ofm[:, ci * 512:(ci + 1) * 512],
                                        start=(ci == 0), stop=(ci == NCH - 1))
                    return last
                p.add("pe", pmm, reads=["l1a_wob", "l1a_ofb"], writes=[pso2n])
                p.add("dve", lambda E, oc=oc, ws=ws, pso2=pso2: E.tensor_tensor(
                    ws.x[:, oc * 512:(oc + 1) * 512], ws.x[:, oc * 512:(oc + 1) * 512],
                    pso2[:, 0:512], op=ALU.add),
                    reads=[pso2n, ws.b("x", oc)], writes=[ws.b("x", oc)])
            store_x_window(ctx, ws, x_out[item], t0)



HOST_PREP = {"l0f": host_l0f, "l3a": host_l3a, "l0a": host_l0a,
             "l1f": host_l1f, "l2a": host_l2a, "l2f": host_l2f,
             "l3f": host_l3f, "l3fadd": lambda inputs: {}, "l1a": host_l1a}
EMIT = {"l0f": emit_l0f, "l3a": emit_l3a, "l0a": emit_l0a,
        "l1f": emit_l1f, "l2a": emit_l2a, "l2f": emit_l2f,
        "l3f": emit_l3f, "l3fadd": emit_l3fadd, "l1a": emit_l1a}

LAUNCH_PLAN = [["l0a"], ["l0f"], ["l1a"], ["l1f"], ["l2a"], ["l2f"], ["l3a"], ["l3f"], ["l3fadd"]]


def _t(x):
    return np.ascontiguousarray(np.swapaxes(np.asarray(x), -1, -2))


def build_and_run(blocks, x_percore, inputs, extra_host=None):
    nc = bass.Bass("TRN2", target_bir_lowering=False)
    ctx = Ctx(nc)
    for b in blocks:
        ctx.hostvals.update(HOST_PREP[b](inputs))
    if extra_host:
        ctx.hostvals.update(extra_host)
    x_in = nc.dram_tensor("x_in", [2, D, T], F32, kind="ExternalInput").ap()
    x_out = nc.dram_tensor("x_out", [2, D, T], F32, kind="ExternalOutput").ap()
    want_y = "l3f" in blocks
    if want_y:
        ctx.y_out_ap = nc.dram_tensor("l3f_y", [D, T // N_CORES], F32,
                                      kind="ExternalOutput").ap()
    n = len(blocks)
    mids = [nc.dram_tensor(f"x_mid{i}", [2, D, T], F32, kind="Internal").ap()
            for i in range(n - 1)]
    chain = [x_in] + mids + [x_out]
    setup_consts(ctx)
    for i, b in enumerate(blocks):
        EMIT[b](ctx, chain[i], chain[i + 1], blk=f"b{i}{b}")
        ctx.p.barrier()
    ctx.p.finalize()
    in_maps = []
    for c in range(N_CORES):
        m = {"x_in": x_percore[c]}
        for name, arr in ctx.host.items():
            m[name] = arr[c] if isinstance(arr, list) else arr
        in_maps.append(m)
    res = run_bass_kernel_spmd(nc, in_maps, core_ids=list(range(N_CORES)))
    outs = [res.results[c]["x_out"] for c in range(N_CORES)]
    if want_y:
        ys = [np.asarray(res.results[c]["l3f_y"]) for c in range(N_CORES)]
        return outs, ys
    return outs, None


def kernel(**inputs):
    x = np.asarray(inputs["L0_attn_x"], np.float32)
    x_fm = _t(x)
    cur = [np.ascontiguousarray(x_fm[2 * c:2 * c + 2]) for c in range(N_CORES)]
    ys = None
    for blocks in LAUNCH_PLAN:
        extra = None
        if "l3f" in blocks:
            xb0 = np.asarray(cur[0])[0]
            extra = {"l3f_xb0": [np.ascontiguousarray(xb0[:, 256 * c:256 * (c + 1)])
                                 for c in range(N_CORES)]}
        if "l3fadd" in blocks:
            yfull = np.concatenate(ys, axis=1)
            z2 = np.zeros((2, D, T), np.float32)
            ya0 = np.stack([yfull, np.zeros((D, T), np.float32)])
            extra = {"l3f_yadd": [ya0] + [z2] * (N_CORES - 1)}
        cur, ys2 = build_and_run(blocks, cur, inputs, extra_host=extra)
        if ys2 is not None:
            ys = ys2
    y = np.concatenate([np.asarray(s) for s in cur], axis=0)
    return _t(y)
